# version 01
# baseline (speedup 1.0000x reference)
import numpy as np
from contextlib import ExitStack
import concourse.bass as bass
import concourse.mybir as mybir
from concourse.bass_utils import run_bass_kernel_spmd

F32 = mybir.dt.float32
BF16 = mybir.dt.bfloat16
I32 = mybir.dt.int32
U32 = mybir.dt.uint32
U8 = mybir.dt.uint8
AF = mybir.ActivationFunctionType
ALU = mybir.AluOpType
AX = mybir.AxisListType

CORES = 8
B = 4
T = 4096
C = 1024
NCH = 8
SW = 128
TOKC = 2048
GROUPS = 4
LNEPS = 1e-5


def strips_of(c):
    return sorted({c, 15 - c, 16 + c, 31 - c})


def owner_order_index(c):
    idx = []
    for b in range(B):
        for s in strips_of(c):
            base = b * T + s * SW
            idx.extend(range(base, base + SW))
    return np.array(idx, dtype=np.int64)



def build_l1_attn():
    nc = bass.Bass("TRN2", target_bir_lowering=False, debug=False)
    xin = nc.dram_tensor("xin", [C, TOKC], F32, kind="ExternalInput").ap()
    ldt = nc.dram_tensor("ldt", [128, NCH], F32, kind="ExternalInput").ap()
    lar = nc.dram_tensor("lar", [128, 128], F32, kind="ExternalInput").ap()
    aim = nc.dram_tensor("aim", [128, 128], F32, kind="ExternalInput").ap()
    brt = nc.dram_tensor("brt", [128, 128], F32, kind="ExternalInput").ap()
    bit = nc.dram_tensor("bit", [128, 128], F32, kind="ExternalInput").ap()
    crt = nc.dram_tensor("crt", [128, 128], F32, kind="ExternalInput").ap()
    cit = nc.dram_tensor("cit", [128, 128], F32, kind="ExternalInput").ap()
    dvt = nc.dram_tensor("dvt", [128, NCH], F32, kind="ExternalInput").ap()
    xout = nc.dram_tensor("xout", [C, TOKC], F32, kind="ExternalOutput").ap()
    dbg_kco = nc.dram_tensor("dbg_kco", [128, 5 * NCH], F32, kind="ExternalOutput").ap()
    dbg_row = nc.dram_tensor("dbg_row", [1, 2 * TOKC], F32, kind="ExternalOutput").ap()
    dbg_ln = nc.dram_tensor("dbg_ln", [128, 1024], F32, kind="ExternalOutput").ap()

    TK = TOKC
    with ExitStack() as ctx:
        e = lambda t: ctx.enter_context(t)
        x32 = e(nc.sbuf_tensor("x32", [128, NCH * TK], F32))
        xbf = e(nc.sbuf_tensor("xbf", [128, NCH * TK], BF16))
        lnb = e(nc.sbuf_tensor("lnb", [128, NCH * TK], BF16))
        prm = e(nc.sbuf_tensor("prm", [128, 6 * 128 + 2 * NCH], F32))
        scr = e(nc.sbuf_tensor("scr", [128, 128 + 2 * 18 * 16], F32))
        kco = e(nc.sbuf_tensor("kco", [128, 4 * NCH + NCH], F32))
        ones_col = e(nc.sbuf_tensor("ones_col", [128, 1], BF16))
        ones_row = e(nc.sbuf_tensor("ones_row", [1, 128], F32))
        srow_m = e(nc.sbuf_tensor("srow_m", [1, TK], F32))
        srow_a = e(nc.sbuf_tensor("srow_a", [1, TK], F32))
        srow_sx = e(nc.sbuf_tensor("srow_sx", [1, TK], F32))
        srow_sxx = e(nc.sbuf_tensor("srow_sxx", [1, TK], F32))
        ab_s = e(nc.sbuf_tensor("ab_s", [128, 2 * 512], F32))
        cbuf = e(nc.sbuf_tensor("cbuf", [128, 4 * 512], F32))
        cns = e(nc.sbuf_tensor("cns", [1, 4], F32))
        p_sa = e(nc.psum_tensor("p_sa", [1, 512], F32))
        p_sb = e(nc.psum_tensor("p_sb", [1, 512], F32))
        p_bc = e(nc.psum_tensor("p_bc", [128, 2 * 512], F32))
        s_ld = e(nc.semaphore("s_ld"))
        s_v = e(nc.semaphore("s_v"))
        s_s = e(nc.semaphore("s_s"))
        s_p = e(nc.semaphore("s_p"))
        s_st = e(nc.semaphore("s_st"))
        nc.const_aps.register(F32, LNEPS, cns[0:1, 0:1])
        block = e(nc.Block())

        P_LDT, P_LAR, P_AIM = 0, NCH, NCH + 128
        P_BR, P_BI, P_CR, P_CI, P_D = (NCH + 256, NCH + 384, NCH + 512,
                                       NCH + 640, NCH + 768)
        K_DT = 4 * NCH

        @block.sync
        def _(sync):
            for ch in range(NCH):
                sync.dma_start(out=x32[:, ch * TK:(ch + 1) * TK],
                               in_=xin[ch * 128:(ch + 1) * 128, :]).then_inc(s_ld, 16)
            for dst_off, src in ((P_LDT, ldt), (P_LAR, lar), (P_AIM, aim),
                                 (P_BR, brt), (P_BI, bit), (P_CR, crt),
                                 (P_CI, cit), (P_D, dvt)):
                w = src.shape[1]
                sync.dma_start(out=prm[:, dst_off:dst_off + w],
                               in_=src[:, :]).then_inc(s_ld, 16)
        LD_ALL = 16 * 16

        @block.vector
        def _(vector):
            vector.wait_ge(s_ld, LD_ALL)
            vector.memset(ones_col[:], 1.0).then_inc(s_v, 1)
            vector.memset(ones_row[:], 1.0).then_inc(s_v, 1)
            vector.memset(cns[0:1, 0:1], LNEPS).then_inc(s_v, 1)
            for ch in range(NCH):
                vector.tensor_copy(xbf[:, ch * TK:(ch + 1) * TK],
                                   x32[:, ch * TK:(ch + 1) * TK]).then_inc(s_v, 1)
        V_CAST = 3 + NCH

        @block.scalar
        def _(scalar):
            scalar.wait_ge(s_ld, LD_ALL)
            for ch in range(NCH):
                scalar.activation(out=kco[:, K_DT + ch:K_DT + ch + 1],
                                  in_=prm[:, P_LDT + ch:P_LDT + ch + 1],
                                  func=AF.Exp).then_inc(s_s, 1)
            scalar.activation(out=scr[:, 0:128], in_=prm[:, P_LAR:P_LAR + 128],
                              func=AF.Exp).then_inc(s_s, 1)
        S_SSM = NCH + 1

        @block.vector
        def _(vector):
            vector.wait_ge(s_s, S_SSM)

            def ssm_ops(ch, half):
                base = 128 + half * 18 * 16
                def sl(i):
                    return scr[:, base + i * 16:base + (i + 1) * 16]
                (dtAr, dtAi, den_r, den_i, num_r, num_i, den2, inv, dAr, dAi,
                 cbr, cbi, pr, pi, t1, t2, t3, t4) = (sl(i) for i in range(18))
                e1 = scr[:, ch * 16:(ch + 1) * 16]
                dt_col = kco[:, K_DT + ch:K_DT + ch + 1]
                A_i = prm[:, P_AIM + ch * 16:P_AIM + (ch + 1) * 16]
                Brc = prm[:, P_BR + ch * 16:P_BR + (ch + 1) * 16]
                Bic = prm[:, P_BI + ch * 16:P_BI + (ch + 1) * 16]
                Crc = prm[:, P_CR + ch * 16:P_CR + (ch + 1) * 16]
                Cic = prm[:, P_CI + ch * 16:P_CI + (ch + 1) * 16]
                TT = lambda o, a, b_, op: (lambda: vector.tensor_tensor(
                    out=o, in0=a, in1=b_, op=op))
                TS = lambda o, a, s1, s2, o0, o1=None: (lambda: vector.tensor_scalar(
                    out=o, in0=a, scalar1=s1, scalar2=s2, op0=o0,
                    **({} if o1 is None else dict(op1=o1))))
                ops = [
                    TS(dtAr, e1, dt_col, -1.0, ALU.mult, ALU.mult),
                    TS(dtAi, A_i, dt_col, None, ALU.mult),
                    TS(den_r, dtAr, -0.5, None, ALU.mult),
                    lambda: vector.tensor_scalar_add(den_r, den_r, 1.0),
                    TS(den_i, dtAi, -0.5, None, ALU.mult),
                    TS(num_r, dtAr, 0.5, None, ALU.mult),
                    lambda: vector.tensor_scalar_add(num_r, num_r, 1.0),
                    TS(num_i, dtAi, 0.5, None, ALU.mult),
                    TT(t1, den_r, den_r, ALU.mult),
                    TT(t2, den_i, den_i, ALU.mult),
                    TT(den2, t1, t2, ALU.add),
                    lambda: vector.reciprocal(out=inv, in_=den2),
                    TT(t1, num_r, den_r, ALU.mult),
                    TT(t2, num_i, den_i, ALU.mult),
                    TT(t3, t1, t2, ALU.add),
                    TT(dAr, t3, inv, ALU.mult),
                    TT(t1, num_i, den_r, ALU.mult),
                    TT(t2, num_r, den_i, ALU.mult),
                    TT(t3, t1, t2, ALU.subtract),
                    TT(dAi, t3, inv, ALU.mult),
                    TT(t1, Crc, Brc, ALU.mult),
                    TT(t2, Cic, Bic, ALU.mult),
                    TT(t3, t1, t2, ALU.subtract),
                    TT(t4, Crc, Bic, ALU.mult),
                    TT(pr, Cic, Brc, ALU.mult),
                    TT(t4, t4, pr, ALU.add),
                    TS(inv, inv, dt_col, None, ALU.mult),
                    TT(t1, t3, den_r, ALU.mult),
                    TT(t2, t4, den_i, ALU.mult),
                    TT(cbr, t1, t2, ALU.add),
                    TT(cbr, cbr, inv, ALU.mult),
                    TT(t1, t4, den_r, ALU.mult),
                    TT(t2, t3, den_i, ALU.mult),
                    TT(cbi, t1, t2, ALU.subtract),
                    TT(cbi, cbi, inv, ALU.mult),
                    lambda: vector.memset(pr, 1.0),
                    lambda: vector.memset(pi, 0.0),
                ]
                for l in range(4):
                    if l > 0:
                        ops += [
                            TT(t1, pr, dAr, ALU.mult),
                            TT(t2, pi, dAi, ALU.mult),
                            TT(t3, t1, t2, ALU.subtract),
                            TT(t1, pr, dAi, ALU.mult),
                            TT(t2, pi, dAr, ALU.mult),
                            TT(pi, t1, t2, ALU.add),
                            (lambda pr=pr, t3=t3: vector.tensor_copy(pr, t3)),
                        ]
                    ops += [
                        TT(t1, cbr, pr, ALU.mult),
                        TT(t2, cbi, pi, ALU.mult),
                        TT(t3, t1, t2, ALU.subtract),
                        (lambda col=ch * 4 + l, t3=t3: vector.tensor_reduce(
                            out=kco[:, col:col + 1], in_=t3, axis=AX.X,
                            op=ALU.add)),
                    ]
                return ops

            for ca in range(0, NCH, 2):
                opsA = ssm_ops(ca, 0)
                opsB = ssm_ops(ca + 1, 1)
                for i in range(max(len(opsA), len(opsB))):
                    if i < len(opsA):
                        opsA[i]()
                    if i < len(opsB):
                        opsB[i]()
            vector.drain()
            vector.tensor_scalar(out=kco[:, 0:4 * NCH], in0=kco[:, 0:4 * NCH],
                                 scalar1=0.5, scalar2=None,
                                 op0=ALU.mult).then_inc(s_v, 1)
        V_KCO = V_CAST + 1

        xsq = lnb
        @block.scalar
        def _(scalar):
            scalar.wait_ge(s_v, V_CAST)
            for ch in range(NCH):
                scalar.activation(out=xsq[:, ch * TK:(ch + 1) * TK],
                                  in_=xbf[:, ch * TK:(ch + 1) * TK],
                                  func=AF.Square).then_inc(s_s, 1)
        S_SQ = S_SSM + NCH

        @block.tensor
        def _(tensor):
            tensor.wait_ge(s_s, S_SQ)
            for g in range(GROUPS):
                if g > 0:
                    tensor.wait_ge(s_v, V_KCO + 2 * g)
                for ch in range(NCH):
                    mm = tensor.matmul(p_sa[0:1, :], ones_col[:, :],
                                       xbf[:, ch * TK + g * 512:ch * TK + (g + 1) * 512],
                                       start=(ch == 0), stop=(ch == NCH - 1))
                    if ch == NCH - 1:
                        mm.then_inc(s_p, 1)
                for ch in range(NCH):
                    mm = tensor.matmul(p_sb[0:1, :], ones_col[:, :],
                                       xsq[:, ch * TK + g * 512:ch * TK + (g + 1) * 512],
                                       start=(ch == 0), stop=(ch == NCH - 1))
                    if ch == NCH - 1:
                        mm.then_inc(s_p, 1)
        P_STATS = 2 * GROUPS

        @block.vector
        def _(vector):
            for g in range(GROUPS):
                vector.wait_ge(s_p, 2 * (g + 1))
                vector.tensor_copy(srow_sx[0:1, g * 512:(g + 1) * 512],
                                   p_sa[0:1, :]).then_inc(s_v, 1)
                vector.tensor_copy(srow_sxx[0:1, g * 512:(g + 1) * 512],
                                   p_sb[0:1, :]).then_inc(s_v, 1)
        V_EVAC = V_KCO + 2 * GROUPS

        @block.vector
        def _(vector):
            vector.tensor_scalar(out=srow_m[0:1, :], in0=srow_sx[0:1, :],
                                 scalar1=1.0 / C, scalar2=None, op0=ALU.mult)
            vector.tensor_tensor(out=srow_a[0:1, :], in0=srow_m[0:1, :],
                                 in1=srow_m[0:1, :], op=ALU.mult)
            vector.scalar_tensor_tensor(out=srow_a[0:1, :], in0=srow_sxx[0:1, :],
                                        scalar=1.0 / C, in1=srow_a[0:1, :],
                                        op0=ALU.mult,
                                        op1=ALU.subtract).then_inc(s_v, 1)
        V_ROWV = V_EVAC + 1

        @block.scalar
        def _(scalar):
            scalar.wait_ge(s_v, V_ROWV)
            scalar.activation(out=srow_a[0:1, :], in_=srow_a[0:1, :], func=AF.Ln,
                              bias=LNEPS)
            scalar.activation(out=srow_a[0:1, :], in_=srow_a[0:1, :], func=AF.Exp,
                              scale=-0.5).then_inc(s_s, 1)
        S_ROWA = S_SQ + 1

        @block.vector
        def _(vector):
            vector.wait_ge(s_s, S_ROWA)
            vector.scalar_tensor_tensor(out=srow_m[0:1, :], in0=srow_m[0:1, :],
                                        scalar=-1.0, in1=srow_a[0:1, :],
                                        op0=ALU.mult,
                                        op1=ALU.mult).then_inc(s_v, 1)
        V_ROWB = V_ROWV + 1

        @block.tensor
        def _(tensor):
            tensor.wait_ge(s_v, V_ROWB)
            for g in range(GROUPS):
                if g > 0:
                    tensor.wait_ge(s_v, V_ROWB + 2 * g)
                tensor.matmul(p_bc[:, 0:512], ones_row[0:1, :],
                              srow_a[0:1, g * 512:(g + 1) * 512],
                              start=True, stop=True).then_inc(s_p, 1)
                tensor.matmul(p_bc[:, 512:1024], ones_row[0:1, :],
                              srow_m[0:1, g * 512:(g + 1) * 512],
                              start=True, stop=True).then_inc(s_p, 1)

        @block.vector
        def _(vector):
            for g in range(GROUPS):
                vector.wait_ge(s_p, P_STATS + 2 * (g + 1))
                vector.tensor_copy(ab_s[:, 0:512], p_bc[:, 0:512]).then_inc(s_v, 1)
                vector.tensor_copy(ab_s[:, 512:1024],
                                   p_bc[:, 512:1024]).then_inc(s_v, 1)
                for ch2 in range(0, NCH, 2):
                    csa = ch2 * TK + g * 512
                    csb = (ch2 + 1) * TK + g * 512
                    vector.tensor_tensor(out=cbuf[:, 0:512], in0=xbf[:, csa:csa + 512],
                                         in1=ab_s[:, 0:512], op=ALU.mult)
                    vector.tensor_tensor(out=cbuf[:, 512:1024],
                                         in0=xbf[:, csb:csb + 512],
                                         in1=ab_s[:, 0:512], op=ALU.mult)
                    vector.tensor_tensor(out=lnb[:, csa:csa + 512], in0=cbuf[:, 0:512],
                                         in1=ab_s[:, 512:1024], op=ALU.add)
                    vector.tensor_tensor(out=lnb[:, csb:csb + 512],
                                         in0=cbuf[:, 512:1024],
                                         in1=ab_s[:, 512:1024], op=ALU.add)
        V_LN = V_ROWB + 2 * GROUPS

        @block.vector
        def _(vector):
            def conv_ops(ch, b_l, half):
                cs = ch * TK + b_l * 512
                acc0 = cbuf[:, half * 1024:half * 1024 + 512]
                acc1 = cbuf[:, half * 1024 + 512:half * 1024 + 1024]
                ops = [lambda: vector.tensor_scalar(
                    out=acc0, in0=lnb[:, cs:cs + 512],
                    scalar1=kco[:, ch * 4:ch * 4 + 1],
                    scalar2=None, op0=ALU.mult)]
                cur, nxt = acc0, acc1
                for m in range(1, min(b_l, 3) + 1):
                    src = ch * TK + (b_l - m) * 512
                    ops.append(lambda src=src, m=m, cur=cur, nxt=nxt:
                               vector.scalar_tensor_tensor(
                                   out=nxt, in0=lnb[:, src:src + 512],
                                   scalar=kco[:, ch * 4 + m:ch * 4 + m + 1],
                                   in1=cur, op0=ALU.mult, op1=ALU.add))
                    cur, nxt = nxt, cur
                ops.append(lambda cur=cur, nxt=nxt: vector.scalar_tensor_tensor(
                    out=nxt, in0=lnb[:, cs:cs + 512],
                    scalar=prm[:, P_D + ch:P_D + ch + 1],
                    in1=cur, op0=ALU.mult, op1=ALU.add))
                ops.append(lambda nxt=nxt: vector.tensor_tensor(
                    out=x32[:, cs:cs + 512], in0=x32[:, cs:cs + 512], in1=nxt,
                    op=ALU.add).then_inc(s_v, 1))
                return ops

            for g in range(GROUPS):
                for ch2 in range(0, NCH, 2):
                    opsA = conv_ops(ch2, g, 0)
                    opsB = conv_ops(ch2 + 1, g, 1)
                    for i in range(max(len(opsA), len(opsB))):
                        if i < len(opsA):
                            opsA[i]()
                        if i < len(opsB):
                            opsB[i]()
        V_CONV = V_LN + GROUPS * NCH

        @block.vector
        def _(vector):
            vector.tensor_copy(cbuf[:, 0:1024], lnb[:, 0:1024]).then_inc(s_v, 1)
            vector.tensor_copy(cbuf[:, 0:1024], lnb[:, 0:1024]).then_inc(s_v, 1)

        @block.sync
        def _(sync):
            sync.wait_ge(s_v, V_CONV + 2)
            for ch in range(NCH):
                sync.dma_start(out=xout[ch * 128:(ch + 1) * 128, :],
                               in_=x32[:, ch * TK:(ch + 1) * TK]).then_inc(s_st, 16)
            sync.dma_start(out=dbg_kco[:, :], in_=kco[:, :]).then_inc(s_st, 16)
            sync.dma_start(out=dbg_row[0:1, 0:TK], in_=srow_a[0:1, :]).then_inc(s_st, 16)
            sync.dma_start(out=dbg_row[0:1, TK:2 * TK], in_=srow_m[0:1, :]).then_inc(s_st, 16)
            sync.dma_start(out=dbg_ln[:, :], in_=cbuf[:, 0:1024]).then_inc(s_st, 16)
            sync.wait_ge(s_st, (NCH + 4) * 16)

    return nc


def stage_l1_attn_inputs(x2_shards, inp):
    def lay(a16):
        return np.ascontiguousarray(
            a16.reshape(NCH, 128, 16).transpose(1, 0, 2).reshape(128, 128))

    ldt = np.ascontiguousarray(np.asarray(inp["L1_attn_log_dt"]).reshape(NCH, 128).T)
    lar = lay(np.asarray(inp["L1_attn_log_A_real"]))
    aim = lay(np.asarray(inp["L1_attn_A_imag"]))
    bb = np.asarray(inp["L1_attn_B"])
    cc = np.asarray(inp["L1_attn_C"])
    brt, bit = lay(bb[:, :, 0]), lay(bb[:, :, 1])
    crt, cit = lay(cc[:, :, 0]), lay(cc[:, :, 1])
    dvt = np.ascontiguousarray(np.asarray(inp["L1_attn_D"]).reshape(NCH, 128).T)
    maps = []
    for c in range(CORES):
        maps.append(dict(xin=x2_shards[c], ldt=ldt, lar=lar, aim=aim, brt=brt,
                         bit=bit, crt=crt, cit=cit, dvt=dvt))
    return maps


def shard_tokens(x_full):
    xf = np.asarray(x_full).reshape(B * T, C)
    return [np.ascontiguousarray(xf[owner_order_index(c)].T) for c in range(CORES)]


def unshard_tokens(shards):
    out = np.zeros((B * T, C), dtype=np.float32)
    for c in range(CORES):
        out[owner_order_index(c)] = np.asarray(shards[c]).T
    return out.reshape(B, T, C)



IFF = 2816
NICH = IFF // 128


def build_l1_ffn():
    nc = bass.Bass("TRN2", target_bir_lowering=False, debug=False)
    xin = nc.dram_tensor("xin", [C, TOKC], F32, kind="ExternalInput").ap()
    wg = nc.dram_tensor("wg", [IFF, C], F32, kind="ExternalInput").ap()
    wu = nc.dram_tensor("wu", [IFF, C], F32, kind="ExternalInput").ap()
    wd = nc.dram_tensor("wd", [C, IFF], F32, kind="ExternalInput").ap()
    clips = nc.dram_tensor("clips", [1, 2], F32, kind="ExternalInput").ap()
    xout = nc.dram_tensor("xout", [C, TOKC], F32, kind="ExternalOutput").ap()
    dbg_scl = nc.dram_tensor("dbg_scl", [128, 64], F32, kind="ExternalOutput").ap()
    dbg_row = nc.dram_tensor("dbg_row", [1, 144], F32, kind="ExternalOutput").ap()
    dbg_ln = nc.dram_tensor("dbg_ln", [128, 1024], F32, kind="ExternalOutput").ap()
    dbg_kxq = nc.dram_tensor("dbg_kxq", [128, 1024], F32, kind="ExternalOutput").ap()
    dbg_h = nc.dram_tensor("dbg_h", [128, 2048], F32, kind="ExternalOutput").ap()
    dbg_sa = nc.dram_tensor("dbg_sa", [1, 2 * TOKC], F32, kind="ExternalOutput").ap()
    dbg_xbf = nc.dram_tensor("dbg_xbf", [128, 1024], F32, kind="ExternalOutput").ap()

    wgq = nc.dram_tensor("wgq", [IFF, C], BF16)
    wuq = nc.dram_tensor("wuq", [IFF, C], BF16)
    wdq = nc.dram_tensor("wdq", [C, IFF], BF16)
    h_hbm = nc.dram_tensor("h_hbm", [IFF, TOKC], BF16)
    cc_in = nc.dram_tensor("cc_in", [1, 16], F32)
    cc_out = nc.dram_tensor("cc_out", [1, 16], F32)
    cc_in2 = nc.dram_tensor("cc_in2", [1, 16], F32)
    cc_out2 = nc.dram_tensor("cc_out2", [1, 16], F32)

    TK = TOKC
    NWT = 22 + 22 + 8
    with ExitStack() as ctx:
        e = lambda t: ctx.enter_context(t)
        ln32 = e(nc.sbuf_tensor("ln32", [128, NCH * TK], F32))
        xbf = e(nc.sbuf_tensor("xbf", [128, NCH * TK], BF16))
        xst = e(nc.sbuf_tensor("xst", [128, 2 * TK], F32))
        wtile = e(nc.sbuf_tensor("wtile", [128, IFF], F32))
        wqt = e(nc.sbuf_tensor("wqt", [128, IFF], BF16))
        wi32 = e(nc.sbuf_tensor("wi32", [128, IFF], I32))
        hi32 = wi32
        idn32 = wi32
        scl = e(nc.sbuf_tensor("scl", [128, 64], F32))
        lhsg = e(nc.sbuf_tensor("lhsg", [128, NCH * 128], BF16))
        lhsu = e(nc.sbuf_tensor("lhsu", [128, NCH * 128], BF16))
        lhsg2 = e(nc.sbuf_tensor("lhsg2", [128, NCH * 128], BF16))
        lhsu2 = e(nc.sbuf_tensor("lhsu2", [128, NCH * 128], BF16))
        lhsd = e(nc.sbuf_tensor("lhsd", [128, NICH * 128], BF16))
        lhsd2 = e(nc.sbuf_tensor("lhsd2", [128, NICH * 128], BF16))
        hq = e(nc.sbuf_tensor("hq", [128, 2 * 512], BF16))
        hsc = e(nc.sbuf_tensor("hsc", [128, 4 * 512], F32))
        silu_t = e(nc.sbuf_tensor("silu_t", [128, 2 * 512], F32))
        ones_col = e(nc.sbuf_tensor("ones_col", [128, 1], BF16))
        ones_row = e(nc.sbuf_tensor("ones_row", [1, 128], F32))
        ident = e(nc.sbuf_tensor("ident", [128, 128], F32))
        srow_m = e(nc.sbuf_tensor("srow_m", [1, TK], F32))
        srow_a = e(nc.sbuf_tensor("srow_a", [1, TK], F32))
        ab_s = e(nc.sbuf_tensor("ab_s", [128, 2 * 512], F32))
        cbuf = e(nc.sbuf_tensor("cbuf", [128, 2 * 512], F32))
        mxc = e(nc.sbuf_tensor("mxc", [128, 16], F32))
        rowmax = e(nc.sbuf_tensor("rowmax", [1, 144], F32))
        cns = e(nc.sbuf_tensor("cns", [1, 4], F32))
        p_sa = e(nc.psum_tensor("p_sa", [1, 512], F32))
        p_sb = e(nc.psum_tensor("p_sb", [1, 512], F32))
        p_bc = e(nc.psum_tensor("p_bc", [128, 2 * 512], F32))
        p_g = e(nc.psum_tensor("p_g", [128, 512], F32))
        p_u = e(nc.psum_tensor("p_u", [128, 512], F32))
        p_t = e(nc.psum_tensor("p_t", [128, 128], F32))
        s_ld = e(nc.semaphore("s_ld"))
        s_lw = e(nc.semaphore("s_lw"))
        s_lg2 = e(nc.semaphore("s_lg2"))
        s_ldn = e(nc.semaphore("s_ldn"))
        s_v = e(nc.semaphore("s_v"))
        s_s = e(nc.semaphore("s_s"))
        s_p = e(nc.semaphore("s_p"))
        s_g = e(nc.semaphore("s_g"))
        s_wq = e(nc.semaphore("s_wq"))
        s_st = e(nc.semaphore("s_st"))
        nc.const_aps.register(F32, LNEPS, cns[0:1, 0:1])
        block = e(nc.Block())

        WQ = [("g", wg, wgq, IFF // 128, C, 0),
              ("u", wu, wuq, IFF // 128, C, 22),
              ("d", wd, wdq, C // 128, IFF, 44)]

        @block.sync
        def _(sync):
            sync.dma_start(out=rowmax[0:1, 128:130], in_=clips[:, :]).then_inc(s_ld, 16)
            for ch in range(NCH):
                if ch >= 2:
                    sync.wait_ge(s_v, 3 + ch - 1)
                sync.dma_start(out=xst[:, (ch % 2) * TK:(ch % 2 + 1) * TK],
                               in_=xin[ch * 128:(ch + 1) * 128, :]).then_inc(s_ld, 16)
        LD_X = 9 * 16

        @block.vector
        def _(vector):
            vector.wait_ge(s_ld, 16)
            vector.memset(ones_col[:], 1.0).then_inc(s_v, 1)
            vector.memset(ones_row[:], 1.0).then_inc(s_v, 1)
            vector.memset(mxc[:, 15:16], 1.0)
            vector.memset(cns[0:1, :], LNEPS).then_inc(s_v, 1)
            for ch in range(NCH):
                vector.wait_ge(s_ld, 16 + 16 * min(ch + 2, NCH))
                vector.tensor_copy(xbf[:, ch * TK:(ch + 1) * TK],
                                   xst[:, (ch % 2) * TK:(ch % 2 + 1) * TK]
                                   ).then_inc(s_v, 1)
        V_CAST = 3 + NCH

        @block.gpsimd
        def _(g):
            g.iota(idn32[:, 1024:1152], pattern=[[-1, 128]], base=0,
                   channel_multiplier=1).then_inc(s_g, 1)

        @block.vector
        def _(vector):
            vector.wait_ge(s_g, 1)
            vector.tensor_scalar(out=ident[:, :], in0=idn32[:, 1024:1152], scalar1=0.0,
                                 scalar2=None, op0=ALU.is_equal).then_inc(s_v, 1)
        V_IDENT = V_CAST + 1

        @block.sync
        def _(sync):
            k = 0
            for tag, src, dst, ntiles, width, soff in WQ:
                for t in range(ntiles):
                    if k >= 1:
                        sync.wait_ge(s_wq, k)
                    sync.dma_start(out=wtile[:, 0:width],
                                   in_=src[t * 128:(t + 1) * 128, :]).then_inc(s_lw, 16)
                    k += 1

        @block.vector
        def _(vector):
            k = 0
            for tag, src, dst, ntiles, width, soff in WQ:
                for t in range(ntiles):
                    vector.wait_ge(s_lw, 16 * (k + 1))
                    if k >= 1:
                        vector.wait_ge(s_g, 1 + 16 * k)
                    wv = wtile[:, 0:width]
                    wiv = wi32[:, 0:width]
                    wqv = wqt[:, 0:width]
                    rm = scl[:, soff + t:soff + t + 1]
                    iv = scl[:, 63:64]
                    vector.reduce_max(out=rm, in_=wv, axis=AX.X,
                                      apply_absolute_value=True)
                    vector.drain()
                    vector.reciprocal(out=iv, in_=rm)
                    vector.drain()
                    vector.tensor_scalar(out=wv, in0=wv, scalar1=iv, scalar2=127.0,
                                         op0=ALU.mult, op1=ALU.mult)
                    vector.tensor_copy(wiv, wv)
                    vector.tensor_copy(wqv, wiv).then_inc(s_wq, 1)
                    k += 1

        @block.gpsimd
        def _(g):
            k = 0
            for tag, src, dst, ntiles, width, soff in WQ:
                for t in range(ntiles):
                    g.wait_ge(s_wq, k + 1)
                    g.dma_start(out=dst[t * 128:(t + 1) * 128, :],
                                in_=wqt[:, 0:width]).then_inc(s_g, 16)
                    k += 1
        G_WST = 1 + NWT * 16

        xsq = ln32
        @block.scalar
        def _(scalar):
            for ch in range(NCH):
                scalar.wait_ge(s_v, 3 + ch + 1)
                scalar.activation(out=xsq[:, ch * TK:(ch + 1) * TK],
                                  in_=xbf[:, ch * TK:(ch + 1) * TK],
                                  func=AF.Square).then_inc(s_s, 1)
        S_SQ = NCH

        @block.tensor
        def _(tensor):
            tensor.wait_ge(s_s, S_SQ)
            tensor.wait_ge(s_v, V_IDENT)
            for g in range(GROUPS):
                if g > 0:
                    tensor.wait_ge(s_v, V_IDENT + 2 * g)
                for ch in range(NCH):
                    mm = tensor.matmul(p_sa[0:1, :], ones_col[:, :],
                                       xbf[:, ch * TK + g * 512:ch * TK + (g + 1) * 512],
                                       start=(ch == 0), stop=(ch == NCH - 1))
                    if ch == NCH - 1:
                        mm.then_inc(s_p, 1)
                for ch in range(NCH):
                    mm = tensor.matmul(p_sb[0:1, :], mxc[:, 15:16],
                                       xsq[:, ch * TK + g * 512:ch * TK + (g + 1) * 512],
                                       start=(ch == 0), stop=(ch == NCH - 1))
                    if ch == NCH - 1:
                        mm.then_inc(s_p, 1)
        P_STATS = 2 * GROUPS

        @block.vector
        def _(vector):
            for g in range(GROUPS):
                vector.wait_ge(s_p, 2 * (g + 1))
                vector.tensor_copy(srow_m[0:1, g * 512:(g + 1) * 512],
                                   p_sa[0:1, :]).then_inc(s_v, 1)
                vector.tensor_copy(srow_a[0:1, g * 512:(g + 1) * 512],
                                   p_sb[0:1, :]).then_inc(s_v, 1)
        V_EVAC = V_IDENT + 2 * GROUPS

        @block.vector
        def _(vector):
            vector.tensor_scalar(out=srow_m[0:1, :], in0=srow_m[0:1, :],
                                 scalar1=1.0 / C, scalar2=None, op0=ALU.mult)
            vector.drain()
            vector.tensor_tensor(out=ab_s[0:1, 0:1024], in0=srow_m[0:1, 0:1024],
                                 in1=srow_m[0:1, 0:1024], op=ALU.mult)
            vector.tensor_tensor(out=cbuf[0:1, 0:1024], in0=srow_m[0:1, 1024:2048],
                                 in1=srow_m[0:1, 1024:2048], op=ALU.mult)
            vector.scalar_tensor_tensor(out=srow_a[0:1, 0:1024],
                                        in0=srow_a[0:1, 0:1024],
                                        scalar=1.0 / C, in1=ab_s[0:1, 0:1024],
                                        op0=ALU.mult, op1=ALU.subtract)
            vector.scalar_tensor_tensor(out=srow_a[0:1, 1024:2048],
                                        in0=srow_a[0:1, 1024:2048],
                                        scalar=1.0 / C, in1=cbuf[0:1, 0:1024],
                                        op0=ALU.mult,
                                        op1=ALU.subtract).then_inc(s_v, 1)
        V_ROWV = V_EVAC + 1

        @block.scalar
        def _(scalar):
            scalar.wait_ge(s_v, V_ROWV)
            scalar.activation(out=srow_a[0:1, :], in_=srow_a[0:1, :], func=AF.Ln,
                              bias=LNEPS)
            scalar.activation(out=srow_a[0:1, :], in_=srow_a[0:1, :], func=AF.Exp,
                              scale=-0.5).then_inc(s_s, 1)
        S_ROWA = S_SQ + 1

        @block.vector
        def _(vector):
            vector.wait_ge(s_s, S_ROWA)
            vector.scalar_tensor_tensor(out=srow_m[0:1, :], in0=srow_m[0:1, :],
                                        scalar=-1.0, in1=srow_a[0:1, :],
                                        op0=ALU.mult, op1=ALU.mult).then_inc(s_v, 1)
        V_ROWB = V_ROWV + 1

        @block.tensor
        def _(tensor):
            tensor.wait_ge(s_v, V_ROWB)
            for g in range(GROUPS):
                if g > 0:
                    tensor.wait_ge(s_v, V_ROWB + 2 * g)
                tensor.matmul(p_bc[:, 0:512], ones_row[0:1, :],
                              srow_a[0:1, g * 512:(g + 1) * 512],
                              start=True, stop=True).then_inc(s_p, 1)
                tensor.matmul(p_bc[:, 512:1024], ones_row[0:1, :],
                              srow_m[0:1, g * 512:(g + 1) * 512],
                              start=True, stop=True).then_inc(s_p, 1)

        @block.vector
        def _(vector):
            for g in range(GROUPS):
                vector.wait_ge(s_p, P_STATS + 2 * (g + 1))
                vector.tensor_copy(ab_s[:, 0:512], p_bc[:, 0:512]).then_inc(s_v, 1)
                vector.tensor_copy(ab_s[:, 512:1024],
                                   p_bc[:, 512:1024]).then_inc(s_v, 1)
                for ch2 in range(0, NCH, 2):
                    csa = ch2 * TK + g * 512
                    csb = (ch2 + 1) * TK + g * 512
                    vector.tensor_tensor(out=cbuf[:, 0:512], in0=xbf[:, csa:csa + 512],
                                         in1=ab_s[:, 0:512], op=ALU.mult)
                    vector.tensor_tensor(out=cbuf[:, 512:1024],
                                         in0=xbf[:, csb:csb + 512],
                                         in1=ab_s[:, 0:512], op=ALU.mult)
                    vector.tensor_tensor(out=ln32[:, csa:csa + 512], in0=cbuf[:, 0:512],
                                         in1=ab_s[:, 512:1024], op=ALU.add)
                    vector.tensor_tensor(out=ln32[:, csb:csb + 512],
                                         in0=cbuf[:, 512:1024],
                                         in1=ab_s[:, 512:1024], op=ALU.add)
            for ch2 in range(0, NCH, 2):
                vector.reduce_max(out=mxc[:, ch2:ch2 + 1],
                                  in_=ln32[:, ch2 * TK:(ch2 + 1) * TK],
                                  axis=AX.X, apply_absolute_value=True)
                vector.reduce_max(out=mxc[:, ch2 + 1:ch2 + 2],
                                  in_=ln32[:, (ch2 + 1) * TK:(ch2 + 2) * TK],
                                  axis=AX.X, apply_absolute_value=True)
            vector.drain()
            vector.tensor_reduce(out=mxc[:, 8:9], in_=mxc[:, 0:8], axis=AX.X,
                                 op=ALU.max).then_inc(s_v, 1)
        V_LN = V_ROWB + 2 * GROUPS + 1

        @block.tensor
        def _(tensor):
            tensor.wait_ge(s_v, V_LN)
            tensor.transpose(p_t[0:1, 0:128], mxc[:, 8:9],
                             ident[:, :]).then_inc(s_p, 1)
        P_TR1 = P_STATS + 2 * GROUPS + 1

        @block.vector
        def _(vector):
            vector.wait_ge(s_p, P_TR1)
            vector.memset(rowmax[0:1, 0:16], 0.0)
            vector.drain()
            vector.reduce_max(out=rowmax[0:1, 0:1], in_=p_t[0:1, 0:128],
                              axis=AX.X).then_inc(s_v, 1)
        V_MX1 = V_LN + 1

        @block.gpsimd
        def _(g):
            g.wait_ge(s_v, V_MX1)
            g.dma_start(out=cc_in[0:1, :], in_=rowmax[0:1, 0:16]).then_inc(s_g, 16)
            g.wait_ge(s_g, G_WST + 16)
            g.collective_compute(
                "AllReduce", ALU.max, replica_groups=[list(range(CORES))],
                ins=[cc_in.ap().opt()], outs=[cc_out.ap().opt()],
            ).then_inc(s_g, 1)
            g.wait_ge(s_g, G_WST + 17)
            g.dma_start(out=rowmax[0:1, 16:32], in_=cc_out[0:1, :]).then_inc(s_g, 16)
        G_MX1 = G_WST + 33

        @block.scalar
        def _(scalar):
            scalar.wait_ge(s_g, G_MX1)
            scalar.activation(out=rowmax[0:1, 32:33], in_=rowmax[0:1, 128:129],
                              func=AF.Sigmoid).then_inc(s_s, 1)
        S_SIG = S_ROWA + 1

        @block.vector
        def _(vector):
            vector.wait_ge(s_s, S_SIG)
            vector.tensor_tensor(out=rowmax[0:1, 33:34], in0=rowmax[0:1, 32:33],
                                 in1=rowmax[0:1, 16:17], op=ALU.mult)
            vector.drain()
            vector.reciprocal(out=rowmax[0:1, 34:35], in_=rowmax[0:1, 33:34])
            vector.drain()
            vector.tensor_scalar(out=rowmax[0:1, 34:35], in0=rowmax[0:1, 34:35],
                                 scalar1=127.0, scalar2=None, op0=ALU.mult)
            vector.tensor_scalar(out=rowmax[0:1, 35:36], in0=rowmax[0:1, 33:34],
                                 scalar1=1.0 / (127.0 * 127.0), scalar2=None,
                                 op0=ALU.mult).then_inc(s_v, 1)
        V_SX = V_MX1 + 1

        @block.tensor
        def _(tensor):
            tensor.wait_ge(s_v, V_SX)
            tensor.matmul(p_t[:, 0:1], ones_row[0:1, :], rowmax[0:1, 34:35],
                          start=True, stop=True)
            tensor.matmul(p_t[:, 1:2], ones_row[0:1, :], rowmax[0:1, 35:36],
                          start=True, stop=True).then_inc(s_p, 1)
        P_BCX = P_TR1 + 1

        @block.vector
        def _(vector):
            vector.wait_ge(s_p, P_BCX)
            vector.tensor_copy(scl[:, 62:63], p_t[:, 0:1])
            vector.tensor_copy(scl[:, 61:62], p_t[:, 1:2]).then_inc(s_v, 1)
            vector.drain()
            vector.tensor_scalar(out=scl[:, 0:22], in0=scl[:, 0:22],
                                 scalar1=scl[:, 61:62], scalar2=None, op0=ALU.mult)
            vector.tensor_scalar(out=scl[:, 22:44], in0=scl[:, 22:44],
                                 scalar1=scl[:, 61:62], scalar2=None,
                                 op0=ALU.mult).then_inc(s_v, 1)
            kxq = xbf
            for ch2 in range(0, NCH, 2):
                for g in range(GROUPS):
                    csa = ch2 * TK + g * 512
                    csb = (ch2 + 1) * TK + g * 512
                    A = hsc[:, 0:512]
                    Bv = hsc[:, 512:1024]
                    vector.tensor_scalar(out=A, in0=ln32[:, csa:csa + 512],
                                         scalar1=scl[:, 62:63], scalar2=None,
                                         op0=ALU.mult)
                    vector.tensor_scalar(out=Bv, in0=ln32[:, csb:csb + 512],
                                         scalar1=scl[:, 62:63], scalar2=None,
                                         op0=ALU.mult)
                    vector.tensor_scalar_max(A, A, -128.0)
                    vector.tensor_scalar_max(Bv, Bv, -128.0)
                    vector.tensor_scalar_min(A, A, 127.0)
                    vector.tensor_scalar_min(Bv, Bv, 127.0)
                    vector.tensor_copy(hi32[:, 0:512], A)
                    vector.tensor_copy(hi32[:, 512:1024], Bv)
                    vector.tensor_copy(kxq[:, csa:csa + 512], hi32[:, 0:512])
                    vector.tensor_copy(kxq[:, csb:csb + 512],
                                       hi32[:, 512:1024]).then_inc(s_v, 1)
        V_KXQ = V_SX + 2 + (NCH // 2) * GROUPS

        kxq = xbf
        @block.sync
        def _(sync):
            sync.wait_ge(s_g, G_WST)
            for i in range(NICH):
                lg = lhsg if i % 2 == 0 else lhsg2
                lu = lhsu if i % 2 == 0 else lhsu2
                if i >= 2:
                    sync.wait_ge(s_p, P_BCX + (i - 1) * GROUPS)
                for h in range(NCH):
                    sync.dma_start(out=lg[:, h * 128:(h + 1) * 128],
                                   in_=wgq[i * 128:(i + 1) * 128,
                                           h * 128:(h + 1) * 128],
                                   transpose=True).then_inc(s_lg2, 16)
                    sync.dma_start(out=lu[:, h * 128:(h + 1) * 128],
                                   in_=wuq[i * 128:(i + 1) * 128,
                                           h * 128:(h + 1) * 128],
                                   transpose=True).then_inc(s_lg2, 16)
        LD_WQ = LD_X + 16 * NWT

        @block.tensor
        def _(tensor):
            tensor.wait_ge(s_v, V_KXQ)
            for i in range(NICH):
                lg = lhsg if i % 2 == 0 else lhsg2
                lu = lhsu if i % 2 == 0 else lhsu2
                tensor.wait_ge(s_lg2, 16 * 2 * NCH * min(i + 2, NICH))
                for g in range(GROUPS):
                    k = i * GROUPS + g
                    if k >= 1:
                        tensor.wait_ge(s_s, S_SIG + k)
                        tensor.wait_ge(s_v, V_KXQ + 1 + 2 * k)
                    for h in range(NCH):
                        tensor.matmul(p_g[:, :], lg[:, h * 128:(h + 1) * 128],
                                      kxq[:, h * TK + g * 512:h * TK + (g + 1) * 512],
                                      start=(h == 0), stop=(h == NCH - 1))
                    for h in range(NCH):
                        mm = tensor.matmul(p_u[:, :], lu[:, h * 128:(h + 1) * 128],
                                           kxq[:, h * TK + g * 512:h * TK + (g + 1) * 512],
                                           start=(h == 0), stop=(h == NCH - 1))
                        if h == NCH - 1:
                            mm.then_inc(s_p, 1)
        P_GU = P_BCX + NICH * GROUPS

        @block.scalar
        def _(scalar):
            for i in range(NICH):
                for g in range(GROUPS):
                    k = i * GROUPS + g
                    scalar.wait_ge(s_p, P_BCX + k + 1)
                    if k >= 2:
                        scalar.wait_ge(s_v, V_KXQ + 1 + 2 * (k - 1))
                    scalar.activation(out=silu_t[:, (k % 2) * 512:(k % 2) * 512 + 512],
                                      in_=p_g[:, :], func=AF.Silu,
                                      scale=scl[:, i:i + 1]).then_inc(s_s, 1)
        S_SILU = S_SIG + NICH * GROUPS

        @block.vector
        def _(vector):
            vector.memset(mxc[:, 9:10], 0.0).then_inc(s_v, 1)
            for i in range(NICH):
                for g in range(GROUPS):
                    k = i * GROUPS + g
                    vector.wait_ge(s_s, S_SIG + k + 1)
                    if k >= 2:
                        vector.wait_ge(s_g, G_MX1 + 16 * k)
                    hv = hsc[:, 1024 + (k % 2) * 512:1024 + (k % 2) * 512 + 512]
                    vector.scalar_tensor_tensor(
                        out=hv, in0=p_u[:, :], scalar=scl[:, 22 + i:23 + i],
                        in1=silu_t[:, (k % 2) * 512:(k % 2) * 512 + 512],
                        op0=ALU.mult, op1=ALU.mult).then_inc(s_v, 1)
                    vector.reduce_max(out=mxc[:, 10 + (k % 2):11 + (k % 2)], in_=hv,
                                      axis=AX.X, apply_absolute_value=True)
                    vector.tensor_copy(hq[:, (k % 2) * 512:(k % 2) * 512 + 512],
                                       hv).then_inc(s_v, 1)
                    vector.tensor_tensor(out=mxc[:, 9:10], in0=mxc[:, 9:10],
                                         in1=mxc[:, 10 + (k % 2):11 + (k % 2)],
                                         op=ALU.max)
        V_H = V_KXQ + 1 + NICH * GROUPS * 2

        @block.gpsimd
        def _(g_):
            for i in range(NICH):
                for g in range(GROUPS):
                    k = i * GROUPS + g
                    g_.wait_ge(s_v, V_KXQ + 1 + 2 * k + 2)
                    g_.dma_start(out=h_hbm[i * 128:(i + 1) * 128,
                                           g * 512:(g + 1) * 512],
                                 in_=hq[:, (k % 2) * 512:(k % 2) * 512 + 512]
                                 ).then_inc(s_g, 16)
        G_HST = G_MX1 + NICH * GROUPS * 16

        @block.vector
        def _(vector):
            vector.drain()
            vector.tensor_copy(mxc[:, 12:13], mxc[:, 9:10]).then_inc(s_v, 1)
        V_HMXC = V_H + 1

        @block.tensor
        def _(tensor):
            tensor.wait_ge(s_v, V_HMXC)
            tensor.transpose(p_t[0:1, 0:128], mxc[:, 12:13],
                             ident[:, :]).then_inc(s_p, 1)
        P_TR2 = P_GU + 1

        @block.vector
        def _(vector):
            vector.wait_ge(s_p, P_TR2)
            vector.memset(rowmax[0:1, 48:64], 0.0)
            vector.drain()
            vector.reduce_max(out=rowmax[0:1, 48:49], in_=p_t[0:1, 0:128],
                              axis=AX.X).then_inc(s_v, 1)
        V_MX2 = V_HMXC + 1

        @block.gpsimd
        def _(g):
            g.wait_ge(s_v, V_MX2)
            g.dma_start(out=cc_in2[0:1, :], in_=rowmax[0:1, 48:64]).then_inc(s_g, 16)
            g.wait_ge(s_g, G_HST + 16)
            g.collective_compute(
                "AllReduce", ALU.max, replica_groups=[list(range(CORES))],
                ins=[cc_in2.ap().opt()], outs=[cc_out2.ap().opt()],
            ).then_inc(s_g, 1)
            g.wait_ge(s_g, G_HST + 17)
            g.dma_start(out=rowmax[0:1, 48:64], in_=cc_out2[0:1, :]).then_inc(s_g, 16)
        G_MX2 = G_HST + 33

        @block.scalar
        def _(scalar):
            scalar.wait_ge(s_g, G_MX2)
            scalar.activation(out=rowmax[0:1, 36:37], in_=rowmax[0:1, 129:130],
                              func=AF.Sigmoid).then_inc(s_s, 1)
        S_SIG2 = S_SILU + 1

        @block.vector
        def _(vector):
            vector.wait_ge(s_s, S_SIG2)
            vector.tensor_tensor(out=rowmax[0:1, 37:38], in0=rowmax[0:1, 36:37],
                                 in1=rowmax[0:1, 48:49], op=ALU.mult)
            vector.drain()
            vector.reciprocal(out=rowmax[0:1, 38:39], in_=rowmax[0:1, 37:38])
            vector.drain()
            vector.tensor_scalar(out=rowmax[0:1, 38:39], in0=rowmax[0:1, 38:39],
                                 scalar1=127.0, scalar2=None, op0=ALU.mult)
            vector.tensor_scalar(out=rowmax[0:1, 39:40], in0=rowmax[0:1, 37:38],
                                 scalar1=1.0 / (127.0 * 127.0), scalar2=None,
                                 op0=ALU.mult).then_inc(s_v, 1)
        V_SH = V_MX2 + 1

        @block.tensor
        def _(tensor):
            tensor.wait_ge(s_v, V_SH)
            tensor.matmul(p_t[:, 0:1], ones_row[0:1, :], rowmax[0:1, 38:39],
                          start=True, stop=True)
            tensor.matmul(p_t[:, 1:2], ones_row[0:1, :], rowmax[0:1, 39:40],
                          start=True, stop=True).then_inc(s_p, 1)
        P_BCH = P_TR2 + 1

        @block.vector
        def _(vector):
            vector.wait_ge(s_p, P_BCH)
            vector.tensor_copy(scl[:, 60:61], p_t[:, 0:1])
            vector.tensor_copy(scl[:, 59:60], p_t[:, 1:2]).then_inc(s_v, 1)
            vector.drain()
            vector.tensor_scalar(out=scl[:, 44:52], in0=scl[:, 44:52],
                                 scalar1=scl[:, 59:60], scalar2=None,
                                 op0=ALU.mult).then_inc(s_v, 1)
        V_SCLD = V_SH + 2

        NPH4 = NICH + 1

        @block.sync
        def _(sync):
            for c_ in range(NCH):
                ld_ = lhsd if c_ % 2 == 0 else lhsd2
                if c_ >= 2:
                    sync.wait_ge(s_p, P_BCH + (c_ - 1) * GROUPS * NICH)
                for i in range(NICH):
                    sync.dma_start(out=ld_[:, i * 128:(i + 1) * 128],
                                   in_=wdq[c_ * 128:(c_ + 1) * 128,
                                           i * 128:(i + 1) * 128],
                                   transpose=True).then_inc(s_ldn, 16)
                if c_ >= 2:
                    sync.wait_ge(s_v, V_SCLD + (c_ - 1) * GROUPS * NPH4)
                    sync.dma_start(out=xout[(c_ - 2) * 128:(c_ - 1) * 128, :],
                                   in_=xst[:, (c_ % 2) * TK:(c_ % 2 + 1) * TK]
                                   ).then_inc(s_st, 16)
                    sync.wait_ge(s_st, (c_ - 1) * 16)
                sync.dma_start(out=xst[:, (c_ % 2) * TK:(c_ % 2 + 1) * TK],
                               in_=xin[c_ * 128:(c_ + 1) * 128, :]).then_inc(s_ldn, 16)

        def xld(c_):
            return 16 * (NICH + 1) * min(c_ + 2, NCH)

        @block.gpsimd
        def _(g_):
            for kk in range(NCH * GROUPS):
                c_, g = kk // GROUPS, kk % GROUPS
                for i in range(NICH):
                    k = kk * NICH + i
                    buf = k % 2
                    if k >= 2:
                        g_.wait_ge(s_p, P_BCH + k - 1)
                    g_.dma_start(out=hq[:, buf * 512:(buf + 1) * 512],
                                 in_=h_hbm[i * 128:(i + 1) * 128,
                                           g * 512:(g + 1) * 512]).then_inc(s_g, 16)

        @block.vector
        def _(vector):
            for kk in range(NCH * GROUPS):
                c_, g = kk // GROUPS, kk % GROUPS
                for i in range(NICH):
                    k = kk * NICH + i
                    buf = k % 2
                    vector.wait_ge(s_g, G_MX2 + 16 * min(k + 2, NCH * GROUPS * NICH))
                    A = hsc[:, buf * 512:(buf + 1) * 512]
                    vector.tensor_scalar(out=A, in0=hq[:, buf * 512:(buf + 1) * 512],
                                         scalar1=scl[:, 60:61], scalar2=None,
                                         op0=ALU.mult)
                    vector.tensor_scalar_max(A, A, -128.0)
                    vector.tensor_scalar_min(A, A, 127.0)
                    vector.tensor_copy(hi32[:, buf * 512:(buf + 1) * 512], A)
                    vector.tensor_copy(hq[:, buf * 512:(buf + 1) * 512],
                                       hi32[:, buf * 512:(buf + 1) * 512]
                                       ).then_inc(s_v, 1)
                vector.wait_ge(s_p, P_BCH + (kk + 1) * NICH)
                vector.wait_ge(s_ldn, xld(c_))
                ps = p_g if kk % 2 == 0 else p_u
                xv = xst[:, (c_ % 2) * TK + g * 512:(c_ % 2) * TK + (g + 1) * 512]
                vector.scalar_tensor_tensor(
                    out=xv, in0=ps[:, :],
                    scalar=scl[:, 44 + c_:45 + c_], in1=xv,
                    op0=ALU.mult, op1=ALU.add).then_inc(s_v, 1)
        V_PH4 = V_SCLD + NCH * GROUPS * NPH4

        def vq(k):
            kk, i = divmod(k, NICH)
            return V_SCLD + kk * NPH4 + i + 1

        @block.tensor
        def _(tensor):
            for kk in range(NCH * GROUPS):
                c_, g = kk // GROUPS, kk % GROUPS
                ld_ = lhsd if c_ % 2 == 0 else lhsd2
                tensor.wait_ge(s_ldn, xld(c_))
                if kk >= 2:
                    tensor.wait_ge(s_v, V_SCLD + (kk - 1) * NPH4)
                for i in range(NICH):
                    k = kk * NICH + i
                    buf = k % 2
                    tensor.wait_ge(s_v, vq(k))
                    tensor.matmul(p_g[:, :] if kk % 2 == 0 else p_u[:, :],
                                  ld_[:, i * 128:(i + 1) * 128],
                                  hq[:, buf * 512:(buf + 1) * 512],
                                  start=(i == 0),
                                  stop=(i == NICH - 1)).then_inc(s_p, 1)
        P_DN = P_BCH + NCH * GROUPS * NICH

        @block.vector
        def _(vector):
            kxq_ = xbf
            vector.tensor_copy(cbuf[:, 0:512], kxq_[:, 0:512])
            vector.tensor_copy(cbuf[:, 512:1024], kxq_[:, 512:1024]).then_inc(s_v, 1)
            vector.tensor_copy(ab_s[:, 0:512], kxq_[:, 1024:1536])
            vector.tensor_copy(ab_s[:, 512:1024], kxq_[:, 1536:2048]).then_inc(s_v, 1)

        @block.gpsimd
        def _(g_):
            g_.wait_ge(s_v, V_PH4 + 2)
            g_.dma_start(out=dbg_h[:, :], in_=h_hbm[0:128, 0:2048]).then_inc(s_g, 16)

        @block.sync
        def _(sync):
            for c_ in (NCH - 2, NCH - 1):
                sync.wait_ge(s_v, V_SCLD + (c_ + 1) * GROUPS * NPH4)
                sync.dma_start(out=xout[c_ * 128:(c_ + 1) * 128, :],
                               in_=xst[:, (c_ % 2) * TK:(c_ % 2 + 1) * TK]
                               ).then_inc(s_st, 16)
            sync.wait_ge(s_v, V_PH4 + 1)
            sync.dma_start(out=dbg_scl[:, :], in_=scl[:, :]).then_inc(s_st, 16)
            sync.dma_start(out=dbg_row[0:1, :], in_=rowmax[0:1, :]).then_inc(s_st, 16)
            sync.dma_start(out=dbg_ln[:, :], in_=ln32[:, 0:1024]).then_inc(s_st, 16)
            sync.dma_start(out=dbg_kxq[:, :], in_=cbuf[:, 0:1024]).then_inc(s_st, 16)
            sync.dma_start(out=dbg_sa[0:1, 0:TK], in_=srow_a[0:1, :]).then_inc(s_st, 16)
            sync.dma_start(out=dbg_sa[0:1, TK:2 * TK], in_=srow_m[0:1, :]).then_inc(s_st, 16)
            sync.dma_start(out=dbg_xbf[:, :], in_=ab_s[:, 0:1024]).then_inc(s_st, 16)
            sync.wait_ge(s_st, (NCH + 7) * 16)
            sync.wait_ge(s_g, G_MX2 + NCH * GROUPS * NICH * 16 + 16)

    return nc


def stage_l1_ffn_inputs(x3_shards, inp):
    wg = np.ascontiguousarray(np.asarray(inp["L1_ffn_w_gate"], dtype=np.float32))
    wu = np.ascontiguousarray(np.asarray(inp["L1_ffn_w_up"], dtype=np.float32))
    wd = np.ascontiguousarray(np.asarray(inp["L1_ffn_w_down"], dtype=np.float32))
    clips = np.stack([np.asarray(inp["L1_ffn_clip_x"]).reshape(()),
                      np.asarray(inp["L1_ffn_clip_down"]).reshape(())]
                     ).reshape(1, 2).astype(np.float32)
    return [dict(xin=x3_shards[c], wg=wg, wu=wu, wd=wd, clips=clips)
            for c in range(CORES)]




class Sem:
    def __init__(self, handle):
        self.h = handle
        self.n = 0

    def inc(self, instr, k=1):
        instr.then_inc(self.h, k)
        self.n += k
        return instr

    def dma(self, instr):
        return self.inc(instr, 16)


def emit_round(eng, x, int_t, cf, sc):
    eng.tensor_copy(int_t, x)
    eng.drain()
    eng.tensor_copy(cf, int_t)
    eng.drain()
    eng.tensor_tensor(out=x, in0=x, in1=cf, op=ALU.subtract)
    eng.drain()
    eng.tensor_scalar(out=sc, in0=x, scalar1=0.5, scalar2=None, op0=ALU.is_ge)
    eng.drain()
    eng.tensor_scalar(out=x, in0=x, scalar1=-0.5, scalar2=None, op0=ALU.is_le)
    eng.drain()
    eng.tensor_tensor(out=cf, in0=cf, in1=sc, op=ALU.add)
    eng.drain()
    eng.tensor_tensor(out=x, in0=cf, in1=x, op=ALU.subtract)
    eng.drain()


TB = T
GRB = TB // 512
NHL = 8
PI = 3.141592653589793
TWO_PI_HI = 6.283203125
TWO_PI_LO = 2 * PI - TWO_PI_HI
LN1E4_HI = 9.2103404
LN1E4_LO = 9.210340371976184 - 9.2103404
PI_HALF_HI = 1.5707963705062866
PI_HALF_LO = (PI / 2) - 1.5707963705062866


def build_l0_attn(stage=3):
    nc = bass.Bass("TRN2", target_bir_lowering=False, debug=False)
    xfull = nc.dram_tensor("xfull", [C, TB], F32, kind="ExternalInput").ap()
    xresi = nc.dram_tensor("xresi", [2048, C], F32, kind="ExternalInput").ap()
    wqk = nc.dram_tensor("wqk", [8 * 128, 8 * 128], F32, kind="ExternalInput").ap()
    wvh = nc.dram_tensor("wvh", [8 * 128, 512], F32, kind="ExternalInput").ap()
    wph = nc.dram_tensor("wph", [4 * 128, 1024], F32, kind="ExternalInput").ap()
    dlt = nc.dram_tensor("dlt", [128, 4], F32, kind="ExternalInput").ap()
    xout = nc.dram_tensor("xout", [2048, C], F32, kind="ExternalOutput").ap()

    kq_scr = nc.dram_tensor("kq_scr", [2 * NHL * 128, TB], BF16)
    v_scr = nc.dram_tensor("v_scr", [TB, 520], BF16)
    rs_in = nc.dram_tensor("rs_in", [TB, C], BF16)
    rs_out = nc.dram_tensor("rs_out", [TB // 2, C], BF16)

    with ExitStack() as ctx:
        e = lambda t: ctx.enter_context(t)
        AR_W = 33 * 1024
        arena = e(nc.sbuf_tensor("arena", [128, AR_W], BF16))
        xst = arena[:, 0:16384].bitcast(F32)
        xbfg = arena[:, 16384:16384 + 8192]
        lnbg = arena[:, 24576:24576 + 8192]
        v_sb = arena[:, 0:32 * 520]
        kbuf = arena[:, 16896:16896 + 2 * TB]
        qbuf = arena[:, 25088:25088 + 2 * TB]
        wqk_sb = e(nc.sbuf_tensor("wqk_sb", [128, 8 * 8 * 128], BF16))
        wv_sb = e(nc.sbuf_tensor("wv_sb", [128, 8 * 512], BF16))
        wp_sb = e(nc.sbuf_tensor("wp_sb", [128, 4 * 1024], BF16))
        at_sb = e(nc.sbuf_tensor("at_sb", [128, 4 * TB], BF16))
        cosf = e(nc.sbuf_tensor("cosf", [128, TB], BF16))
        sinf = e(nc.sbuf_tensor("sinf", [128, TB], BF16))
        trigs = at_sb[:, 0:3072].bitcast(F32)
        tf32 = at_sb[:, 3072:5120].bitcast(F32)
        trci = at_sb[:, 5120:6144].bitcast(I32)
        rsc1 = at_sb[:, 6144:7168].bitcast(F32)
        rsc2 = at_sb[:, 7168:8192].bitcast(F32)
        itri = e(nc.sbuf_tensor("itri", [128, 129], I32))
        dcs = e(nc.sbuf_tensor("dcs", [128, 16], F32))
        tri = e(nc.sbuf_tensor("tri", [128, 4 * 512], BF16))
        sp1 = e(nc.sbuf_tensor("sp1", [128, 2 * 512], F32))
        sp2 = e(nc.sbuf_tensor("sp2", [128, 2 * 512], F32))
        sq4 = e(nc.sbuf_tensor("sq4", [128, 4 * 512], F32))
        lnsc = e(nc.sbuf_tensor("lnsc", [128, 2 * 512], F32))
        mu = e(nc.sbuf_tensor("mu", [128, 2 * 512], F32))
        msq = e(nc.sbuf_tensor("msq", [128, 2 * 512], F32))
        rot = e(nc.sbuf_tensor("rot", [128, 4 * 512], BF16))
        c1s1 = e(nc.sbuf_tensor("c1s1", [128, 2 * 512], F32))
        vtile = e(nc.sbuf_tensor("vtile", [128, 2 * 520], BF16))
        srow_m = e(nc.sbuf_tensor("srow_m", [1, 512], F32))
        srow_a = e(nc.sbuf_tensor("srow_a", [1, 512], F32))
        srow_t = e(nc.sbuf_tensor("srow_t", [1, 512], F32))
        ab_s = e(nc.sbuf_tensor("ab_s", [128, 2 * 512], F32))
        nrow = e(nc.sbuf_tensor("nrow", [1, 8], F32))
        negc = e(nc.sbuf_tensor("negc", [128, 1], F32))
        ones_rowb = e(nc.sbuf_tensor("ones_rowb", [1, 128], BF16))
        probs = e(nc.sbuf_tensor("probs", [128, 2 * 512], BF16))
        araw = e(nc.sbuf_tensor("araw", [64, 2 * 512], F32))
        drow = sq4[0:1, 0:1024]
        prjev = e(nc.sbuf_tensor("prjev", [128, 2 * 512], BF16))
        rsld = mu[:, 0:1024].bitcast(BF16)[:, 0:2048]
        xrld1 = sp1
        xrld2 = sp2
        ones_col = e(nc.sbuf_tensor("ones_col", [128, 1], BF16))
        onesf_col = e(nc.sbuf_tensor("onesf_col", [128, 1], F32))
        ones_row = e(nc.sbuf_tensor("ones_row", [1, 128], F32))
        cns = e(nc.sbuf_tensor("cns", [128, 8], F32))
        p_sa = e(nc.psum_tensor("p_sa", [1, 512], F32))
        p_sb = e(nc.psum_tensor("p_sb", [1, 512], F32))
        p_bc = e(nc.psum_tensor("p_bc", [128, 2 * 512], F32))
        p_q = e(nc.psum_tensor("p_q", [128, 2 * 512], F32))
        p_pv = e(nc.psum_tensor("p_pv", [65, 512], F32))
        p_n = e(nc.psum_tensor("p_n", [1, 512], F32))
        s_lx = Sem(e(nc.semaphore("s_lx")))
        s_dl = Sem(e(nc.semaphore("s_dl")))
        s_gw = Sem(e(nc.semaphore("s_gw")))
        s_v = Sem(e(nc.semaphore("s_v")))
        s_s = Sem(e(nc.semaphore("s_s")))
        s_p = Sem(e(nc.semaphore("s_p")))
        s_g = Sem(e(nc.semaphore("s_g")))
        s_kq = Sem(e(nc.semaphore("s_kq")))
        s_vw = Sem(e(nc.semaphore("s_vw")))
        s_lk = Sem(e(nc.semaphore("s_lk")))
        s_lv = Sem(e(nc.semaphore("s_lv")))
        s_cc = Sem(e(nc.semaphore("s_cc")))
        s_pw = Sem(e(nc.semaphore("s_pw")))
        s_ro = Sem(e(nc.semaphore("s_ro")))
        s_st = Sem(e(nc.semaphore("s_st")))
        nc.const_aps.register(F32, LNEPS, cns[:, 0:1])
        nc.const_aps.register(F32, 1.0, cns[:, 1:2])
        nc.const_aps.register(F32, PI_HALF_HI, cns[:, 2:3])
        nc.const_aps.register(F32, 0.125, cns[:, 3:4])
        block = e(nc.Block())

        def xstv(par, cc):
            return xst[:, (par * 8 + cc) * 512:(par * 8 + cc + 1) * 512]

        def xbv(par, cc):
            return xbfg[:, (par * 8 + cc) * 512:(par * 8 + cc + 1) * 512]

        def lnv(par, cc):
            return lnbg[:, (par * 8 + cc) * 512:(par * 8 + cc + 1) * 512]

        @block.gpsimd
        def _(g):
            for fc in range(8):
                s_gw.dma(g.dma_start(out=wqk_sb[:, fc * 1024:(fc + 1) * 1024],
                                     in_=wqk[fc * 128:(fc + 1) * 128, :]))
            for cc in range(8):
                s_gw.dma(g.dma_start(out=wv_sb[:, cc * 512:(cc + 1) * 512],
                                     in_=wvh[cc * 128:(cc + 1) * 128, :]))
            for cc in range(4):
                s_gw.dma(g.dma_start(out=wp_sb[:, cc * 1024:(cc + 1) * 1024],
                                     in_=wph[cc * 128:(cc + 1) * 128, :]))
        GW_ALL = s_gw.n

        @block.gpsimd
        def _(g):
            g.iota(itri[:, 0:128], pattern=[[1, 128]], base=0,
                   channel_multiplier=-1)
            s_g.inc(g.iota(itri[:, 128:129], pattern=[[0, 1]], base=0,
                           channel_multiplier=1))
        G_IOTA = s_g.n

        V0 = 16
        PERG = 8 + 1 + 1 + 1 + 4 \
            + ((8 + 8 + 8) if stage >= 0.8 else 0) \
            + (4 if stage >= 0.9 else 0)
        CAST_GUARD = [V0 + g * PERG + 8 for g in range(GRB)]

        def emit_x_load(g_i):
            @block.sync
            def _(sync, g_i=g_i):
                par = g_i % 2
                if g_i >= 2:
                    sync.wait_ge(s_v.h, CAST_GUARD[g_i - 2])
                for cc in range(8):
                    s_lx.dma(sync.dma_start(
                        out=xstv(par, cc),
                        in_=xfull[cc * 128:(cc + 1) * 128,
                                  g_i * 512:(g_i + 1) * 512]))

        @block.sync
        def _(sync):
            s_dl.dma(sync.dma_start(out=dcs[:, 8:12], in_=dlt[:, :]))

        emit_x_load(0)
        emit_x_load(1)

        def lxw(g_i):
            return 16 * 8 * min(g_i + 2, GRB)


        @block.vector
        def _(vector):
            vector.wait_ge(s_g.h, G_IOTA)
            vector.memset(ones_rowb[:], 1.0)
            s_v.inc(vector.memset(ones_col[:], 1.0))
            s_v.inc(vector.memset(onesf_col[:], 1.0))
            s_v.inc(vector.memset(ones_row[:], 1.0))
            vector.memset(cns[:, 0:1], LNEPS)
            vector.memset(cns[:, 1:2], 1.0)
            vector.memset(cns[:, 2:3], PI_HALF_HI)
            s_v.inc(vector.memset(cns[:, 3:4], 0.125))
            for m in range(4):
                for blk in range(4):
                    dst = tri[:, m * 512 + blk * 128:m * 512 + (blk + 1) * 128]
                    if blk < m:
                        vector.memset(dst, 0.0)
                    elif blk > m:
                        vector.memset(dst, 1.0)
                    else:
                        vector.tensor_scalar(out=dst, in0=itri[:, 0:128],
                                             scalar1=0.0, scalar2=None,
                                             op0=ALU.is_ge)
            s_v.inc(vector.memset(dcs[:, 15:16], 0.0))
        assert s_v.n == 5

        @block.vector
        def _(vector):
            vector.wait_ge(s_dl.h, 16)
            vector.tensor_scalar_max(dcs[:, 8:12], dcs[:, 8:12], -2 * PI)
            vector.drain()
            vector.tensor_scalar_min(dcs[:, 8:12], dcs[:, 8:12], 0.0)
            vector.drain()
            s_v.inc(vector.tensor_scalar(out=dcs[:, 0:4], in0=dcs[:, 8:12],
                                         scalar1=0.5, scalar2=None,
                                         op0=ALU.mult))
        V_DHALF = s_v.n

        @block.scalar
        def _(scalar):
            scalar.wait_ge(s_v.h, V_DHALF)
            scalar.activation(out=dcs[:, 4:8], in_=dcs[:, 0:4], func=AF.Sin)
            scalar.drain()
            s_s.inc(scalar.activation(out=dcs[:, 0:4], in_=dcs[:, 0:4],
                                      func=AF.Sin, bias=PI_HALF_HI))
        S_DCS = s_s.n

        @block.vector
        def _(vector):
            vector.wait_ge(s_s.h, S_DCS)
            vector.tensor_tensor(out=dcs[:, 8:12], in0=dcs[:, 4:8],
                                 in1=dcs[:, 4:8], op=ALU.mult)
            vector.tensor_tensor(out=dcs[:, 12:16], in0=dcs[:, 4:8],
                                 in1=dcs[:, 0:4], op=ALU.mult)
            vector.drain()
            vector.tensor_scalar(out=dcs[:, 8:12], in0=dcs[:, 8:12],
                                 scalar1=-2.0, scalar2=None, op0=ALU.mult)
            vector.tensor_scalar(out=dcs[:, 12:16], in0=dcs[:, 12:16],
                                 scalar1=2.0, scalar2=None, op0=ALU.mult)
            vector.drain()
            s_v.inc(vector.tensor_scalar_add(dcs[:, 8:12], dcs[:, 8:12], 1.0))
            p_col = itri[:, 128:129]
            vector.tensor_scalar(out=trigs[:, 0:1], in0=p_col, scalar1=64.0,
                                 scalar2=None, op0=ALU.is_ge)
            vector.drain()
            vector.scalar_tensor_tensor(out=trigs[:, 1:2], in0=trigs[:, 0:1],
                                        scalar=-64.0, in1=p_col,
                                        op0=ALU.mult, op1=ALU.add)
            vector.drain()
            vector.tensor_scalar(out=trigs[:, 2:3], in0=trigs[:, 1:2],
                                 scalar1=-LN1E4_HI / 64.0, scalar2=None,
                                 op0=ALU.mult)
            vector.drain()
            s_v.inc(vector.scalar_tensor_tensor(
                out=trigs[:, 0:1], in0=trigs[:, 1:2],
                scalar=-LN1E4_LO / 64.0, in1=trigs[:, 2:3],
                op0=ALU.mult, op1=ALU.add))
        V_DCS = s_v.n

        @block.scalar
        def _(scalar):
            scalar.wait_ge(s_v.h, V_DCS)
            s_s.inc(scalar.activation(out=dcs[:, 14:15], in_=trigs[:, 0:1],
                                      func=AF.Exp))
        S_IVF = s_s.n

        @block.gpsimd
        def _(g):
            for blk in range(GRB):
                if blk >= 2:
                    g.wait_ge(s_v.h, V_DCS + blk - 1)
                s_g.inc(g.iota(tf32[:, (blk % 2) * 512:(blk % 2 + 1) * 512],
                               pattern=[[1, 512]], base=blk * 512,
                               channel_multiplier=0,
                               allow_small_or_imprecise_dtypes=True))
        G_TRIG0 = G_IOTA

        @block.vector
        def _(vector):
            vector.wait_ge(s_s.h, S_IVF)
            inv = dcs[:, 14:15]
            for blk in range(GRB):
                vector.wait_ge(s_g.h, G_TRIG0 + blk + 1)
                if blk >= 1:
                    vector.wait_ge(s_s.h, S_IVF + blk)
                tcol = tf32[:, (blk % 2) * 512:(blk % 2 + 1) * 512]
                f = trigs[:, 0:512]
                m = trigs[:, 512:1024]
                fr = trigs[:, 1024:1536]
                vector.tensor_scalar(out=f, in0=tcol, scalar1=inv,
                                     scalar2=None, op0=ALU.mult)
                vector.drain()
                vector.tensor_scalar(out=m, in0=f, scalar1=1.0 / (2 * PI),
                                     scalar2=None, op0=ALU.mult)
                vector.drain()
                emit_round(vector, m, trci[:, 0:512], rsc1[:, 0:512],
                           rsc2[:, 0:512])
                vector.scalar_tensor_tensor(out=fr, in0=m, scalar=-TWO_PI_HI,
                                            in1=f, op0=ALU.mult, op1=ALU.add)
                vector.drain()
                vector.scalar_tensor_tensor(out=fr, in0=m, scalar=-TWO_PI_LO,
                                            in1=fr, op0=ALU.mult, op1=ALU.add)
                vector.tensor_scalar(out=m, in0=f, scalar1=1.0 / (2 * PI),
                                     scalar2=None, op0=ALU.mult)
                vector.drain()
                vector.tensor_scalar_add(m, m, 0.25)
                vector.drain()
                emit_round(vector, m, trci[:, 0:512], rsc1[:, 0:512],
                           rsc2[:, 0:512])
                vector.scalar_tensor_tensor(out=f, in0=m, scalar=-TWO_PI_HI,
                                            in1=f, op0=ALU.mult, op1=ALU.add)
                vector.drain()
                vector.scalar_tensor_tensor(out=f, in0=m, scalar=-TWO_PI_LO,
                                            in1=f, op0=ALU.mult, op1=ALU.add)
                vector.drain()
                vector.tensor_scalar_add(f, f, PI_HALF_HI)
                vector.drain()
                s_v.inc(vector.tensor_scalar_add(f, f, PI_HALF_LO))
        V_TRIG0 = V_DCS
        V_TRIG = s_v.n
        assert V_TRIG == V0, (V_TRIG, V0)

        @block.scalar
        def _(scalar):
            for blk in range(GRB):
                scalar.wait_ge(s_v.h, V_TRIG0 + blk + 1)
                scalar.activation(out=sinf[:, blk * 512:(blk + 1) * 512],
                                  in_=trigs[:, 1024:1536], func=AF.Sin)
                s_s.inc(scalar.activation(out=cosf[:, blk * 512:(blk + 1) * 512],
                                          in_=trigs[:, 0:512], func=AF.Sin))
        S_TRIG = s_s.n

        if stage < 0.5:
            @block.gpsimd
            def _(g):
                g.wait_ge(s_s.h, S_TRIG)
                g.wait_ge(s_gw.h, GW_ALL)
                g.wait_ge(s_lx.h, 16 * 16)
                g.dma_start(out=xout[:, :], in_=xresi[:, :]).then_inc(s_g.h, 16)
                g.wait_ge(s_g.h, s_g.n + 16)
            return nc

        mu_done = []
        msq_done = []
        nm_done = []
        nrm_done = []
        rot_done = []
        ve_done = []
        unit = [0]
        vunit = [0]
        prev_rv = [None]
        prev_stx = [None]

        for g_i in range(GRB):
            par = g_i % 2

            @block.vector
            def _(vector, g_i=g_i, par=par):
                vector.wait_ge(s_lx.h, lxw(g_i))
                for cc in range(8):
                    s_v.inc(vector.tensor_copy(xbv(par, cc), xstv(par, cc)))
            CAST_G = s_v.n
            assert CAST_G == CAST_GUARD[g_i], (g_i, CAST_G, CAST_GUARD[g_i])

            @block.tensor
            def _(tensor, CAST_G=CAST_G, g_i=g_i, par=par, pr=prev_rv[0]):
                tensor.wait_ge(s_v.h, CAST_G)
                if g_i > 0:
                    tensor.wait_ge(s_v.h, pr)
                for cc in range(8):
                    mm = tensor.matmul(p_sa[0:1, :], ones_col[:, :],
                                       xbv(par, cc),
                                       start=(cc == 0), stop=(cc == 7))
                    if cc == 7:
                        s_p.inc(mm)
            STX_G = s_p.n

            @block.scalar
            def _(scalar, CAST_G=CAST_G, g_i=g_i, par=par, ps=prev_stx[0]):
                scalar.wait_ge(s_v.h, CAST_G)
                if g_i > 0:
                    scalar.wait_ge(s_p.h, ps)
                for cc in range(8):
                    s_s.inc(scalar.activation(
                        out=sq4[:, (cc % 4) * 512:(cc % 4 + 1) * 512],
                        in_=xbv(par, cc), func=AF.Square))
            SQ_G0 = s_s.n - 8

            @block.tensor
            def _(tensor, SQ_G0=SQ_G0):
                for cc in range(8):
                    tensor.wait_ge(s_s.h, SQ_G0 + cc + 1)
                    mm = tensor.matmul(p_sb[0:1, :], onesf_col[:, :],
                                       sq4[:, (cc % 4) * 512:(cc % 4 + 1) * 512],
                                       start=(cc == 0), stop=(cc == 7))
                    if cc == 7:
                        s_p.inc(mm)
            ST_G = s_p.n
            prev_stx[0] = ST_G

            @block.vector
            def _(vector, ST_G=ST_G):
                vector.wait_ge(s_p.h, ST_G)
                vector.tensor_scalar(out=srow_m[0:1, :], in0=p_sa[0:1, :],
                                     scalar1=1.0 / C, scalar2=None,
                                     op0=ALU.mult)
                vector.drain()
                vector.tensor_tensor(out=srow_t[0:1, :], in0=srow_m[0:1, :],
                                     in1=srow_m[0:1, :], op=ALU.mult)
                vector.drain()
                s_v.inc(vector.scalar_tensor_tensor(
                    out=srow_a[0:1, :], in0=p_sb[0:1, :], scalar=1.0 / C,
                    in1=srow_t[0:1, :], op0=ALU.mult, op1=ALU.subtract))
            RV_G = s_v.n
            prev_rv[0] = RV_G

            @block.scalar
            def _(scalar, RV_G=RV_G):
                scalar.wait_ge(s_v.h, RV_G)
                scalar.activation(out=srow_a[0:1, :], in_=srow_a[0:1, :],
                                  func=AF.Ln, bias=LNEPS)
                scalar.drain()
                s_s.inc(scalar.activation(out=srow_a[0:1, :],
                                          in_=srow_a[0:1, :],
                                          func=AF.Exp, scale=-0.5))
            RA_G = s_s.n

            @block.vector
            def _(vector, RA_G=RA_G):
                vector.wait_ge(s_s.h, RA_G)
                s_v.inc(vector.scalar_tensor_tensor(
                    out=srow_m[0:1, :], in0=srow_m[0:1, :], scalar=-1.0,
                    in1=srow_a[0:1, :], op0=ALU.mult, op1=ALU.mult))
            RB_G = s_v.n

            @block.tensor
            def _(tensor, RB_G=RB_G):
                tensor.wait_ge(s_v.h, RB_G)
                tensor.matmul(p_bc[:, 0:512], ones_row[0:1, :], srow_a[0:1, :],
                              start=True, stop=True)
                s_p.inc(tensor.matmul(p_bc[:, 512:1024], ones_row[0:1, :],
                                      srow_m[0:1, :], start=True, stop=True))
            BC_G = s_p.n

            @block.vector
            def _(vector, BC_G=BC_G, par=par):
                vector.wait_ge(s_p.h, BC_G)
                vector.tensor_copy(ab_s[:, 0:512], p_bc[:, 0:512])
                s_v.inc(vector.tensor_copy(ab_s[:, 512:1024],
                                           p_bc[:, 512:1024]))
                for cc2 in range(0, 8, 2):
                    vector.tensor_tensor(out=lnsc[:, 0:512],
                                         in0=xbv(par, cc2),
                                         in1=ab_s[:, 0:512], op=ALU.mult)
                    vector.tensor_tensor(out=lnsc[:, 512:1024],
                                         in0=xbv(par, cc2 + 1),
                                         in1=ab_s[:, 0:512], op=ALU.mult)
                    vector.tensor_tensor(out=lnv(par, cc2), in0=lnsc[:, 0:512],
                                         in1=ab_s[:, 512:1024], op=ALU.add)
                    s_v.inc(vector.tensor_tensor(out=lnv(par, cc2 + 1),
                                                 in0=lnsc[:, 512:1024],
                                                 in1=ab_s[:, 512:1024],
                                                 op=ALU.add))
            LN_G = s_v.n

            for fc in (range(8) if stage >= 0.8 else []):
                is_q = fc < 4
                u = unit[0]
                ppar = u % 2

                @block.tensor
                def _(tensor, fc=fc, LN_G=LN_G, ppar=ppar, u=u, par=par):
                    tensor.wait_ge(s_v.h, LN_G)
                    if u == 0:
                        tensor.wait_ge(s_gw.h, GW_ALL)
                    if u >= 2:
                        tensor.wait_ge(s_v.h, mu_done[u - 2])
                    for cc in range(8):
                        mm = tensor.matmul(
                            p_q[:, ppar * 512:(ppar + 1) * 512],
                            wqk_sb[:, (fc * 8 + cc) * 128:(fc * 8 + cc + 1) * 128],
                            lnv(par, cc), start=(cc == 0), stop=(cc == 7))
                        if cc == 7:
                            s_p.inc(mm)
                MM_U = s_p.n

                @block.scalar
                def _(scalar, MM_U=MM_U, ppar=ppar, u=u):
                    scalar.wait_ge(s_p.h, MM_U)
                    if u >= 2:
                        scalar.wait_ge(s_v.h, mu_done[u - 2])
                    pq = p_q[:, ppar * 512:(ppar + 1) * 512]
                    sa = sp1[:, ppar * 512:(ppar + 1) * 512]
                    sb = sp2[:, ppar * 512:(ppar + 1) * 512]
                    scalar.activation(out=sa, in_=pq, func=AF.Abs)
                    scalar.activation(out=sb, in_=sa, func=AF.Exp, scale=-1.0)
                    s_s.inc(scalar.activation(out=sa, in_=sb, func=AF.Ln,
                                              bias=1.0))
                SP_U = s_s.n

                @block.vector
                def _(vector, SP_U=SP_U, ppar=ppar, u=u):
                    vector.wait_ge(s_s.h, SP_U)
                    if u >= 2:
                        vector.wait_ge(s_s.h, msq_done[u - 2])
                    s_v.inc(vector.scalar_tensor_tensor(
                        out=mu[:, ppar * 512:(ppar + 1) * 512],
                        in0=p_q[:, ppar * 512:(ppar + 1) * 512], scalar=0.0,
                        in1=sp1[:, ppar * 512:(ppar + 1) * 512],
                        op0=ALU.max, op1=ALU.add))
                MU_U = s_v.n
                mu_done.append(MU_U)

                @block.scalar
                def _(scalar, MU_U=MU_U, ppar=ppar, u=u):
                    scalar.wait_ge(s_v.h, MU_U)
                    if u >= 2:
                        scalar.wait_ge(s_p.h, nm_done[u - 2])
                    s_s.inc(scalar.activation(
                        out=msq[:, ppar * 512:(ppar + 1) * 512],
                        in_=mu[:, ppar * 512:(ppar + 1) * 512],
                        func=AF.Square))
                MSQ_U = s_s.n
                msq_done.append(MSQ_U)

                @block.vector
                def _(vector, fc=fc, g_i=g_i, is_q=is_q, ppar=ppar, u=u):
                    if g_i == 0 and fc == 0:
                        vector.wait_ge(s_s.h, S_TRIG)
                    if u >= 2:
                        vector.wait_ge(s_kq.h, 64 * (u - 1))
                    muv = mu[:, ppar * 512:(ppar + 1) * 512]
                    cv = cosf[:, g_i * 512:(g_i + 1) * 512]
                    sv = sinf[:, g_i * 512:(g_i + 1) * 512]
                    r0 = rot[:, (2 * ppar) * 512:(2 * ppar + 1) * 512]
                    r1 = rot[:, (2 * ppar + 1) * 512:(2 * ppar + 2) * 512]
                    if is_q:
                        vector.tensor_tensor(out=r0, in0=muv, in1=cv,
                                             op=ALU.mult)
                        s_v.inc(vector.tensor_tensor(out=r1, in0=muv, in1=sv,
                                                     op=ALU.mult))
                    else:
                        kc = fc - 4
                        cd = dcs[:, 8 + kc:9 + kc]
                        sd = dcs[:, 12 + kc:13 + kc]
                        a_ = c1s1[:, 0:512]
                        b_ = c1s1[:, 512:1024]
                        vector.tensor_scalar(out=a_, in0=cv, scalar1=cd,
                                             scalar2=None, op0=ALU.mult)
                        vector.tensor_scalar(out=b_, in0=sv, scalar1=sd,
                                             scalar2=None, op0=ALU.mult)
                        vector.tensor_tensor(out=a_, in0=a_, in1=b_,
                                             op=ALU.subtract)
                        vector.tensor_scalar(out=b_, in0=sv, scalar1=cd,
                                             scalar2=None, op0=ALU.mult)
                        vector.scalar_tensor_tensor(out=b_, in0=cv, scalar=sd,
                                                    in1=b_, op0=ALU.mult,
                                                    op1=ALU.add)
                        vector.tensor_tensor(out=r0, in0=muv, in1=a_,
                                             op=ALU.mult)
                        s_v.inc(vector.tensor_tensor(out=r1, in0=muv, in1=b_,
                                                     op=ALU.mult))
                ROT_U = s_v.n

                @block.tensor
                def _(tensor, MSQ_U=MSQ_U, ppar=ppar, u=u):
                    tensor.wait_ge(s_s.h, MSQ_U)
                    if u >= 1:
                        tensor.wait_ge(s_v.h, nrm_done[u - 1])
                    msv = msq[:, ppar * 512:(ppar + 1) * 512]
                    s_p.inc(tensor.matmul(p_n[0:1, :], onesf_col[:, 0:1],
                                          msv[:, :], start=True, stop=True))
                NM_U = s_p.n
                nm_done.append(NM_U)

                @block.vector
                def _(vector, NM_U=NM_U, fc=fc, g_i=g_i):
                    vector.wait_ge(s_p.h, NM_U)
                    idx = 0 if fc < 4 else 1
                    vector.reduce_max(out=nrow[0:1, 4:5], in_=p_n[0:1, :],
                                      axis=AX.X)
                    vector.drain()
                    if g_i == 0 and fc in (0, 4):
                        s_v.inc(vector.tensor_copy(nrow[0:1, idx:idx + 1],
                                                   nrow[0:1, 4:5]))
                    else:
                        s_v.inc(vector.tensor_tensor(
                            out=nrow[0:1, idx:idx + 1],
                            in0=nrow[0:1, idx:idx + 1],
                            in1=nrow[0:1, 4:5], op=ALU.max))
                nrm_done.append(s_v.n)

                @block.sync
                def _(sync, fc=fc, g_i=g_i, ROT_U=ROT_U, ppar=ppar, is_q=is_q):
                    sync.wait_ge(s_v.h, ROT_U)
                    r0 = rot[:, (2 * ppar) * 512:(2 * ppar + 1) * 512]
                    r1 = rot[:, (2 * ppar + 1) * 512:(2 * ppar + 2) * 512]
                    base = NHL * 128 if is_q else 0
                    hl0 = (fc % 4) * 2
                    for src, hl, roff in ((r0[0:64, :], hl0, 0),
                                          (r0[64:128, :], hl0 + 1, 0),
                                          (r1[0:64, :], hl0, 64),
                                          (r1[64:128, :], hl0 + 1, 64)):
                        row = base + hl * 128 + roff
                        s_kq.dma(sync.dma_start(
                            out=kq_scr[row:row + 64,
                                       g_i * 512:(g_i + 1) * 512],
                            in_=src))
                unit[0] += 1

            for tt4 in (range(4) if stage >= 0.9 else []):
                vu = vunit[0]
                ppar = vu % 2

                @block.tensor
                def _(tensor, g_i=g_i, tt4=tt4, LN_G=LN_G, ppar=ppar, vu=vu,
                      par=par):
                    tensor.wait_ge(s_v.h, LN_G)
                    if vu >= 2:
                        tensor.wait_ge(s_v.h, ve_done[vu - 2])
                    for cc in range(8):
                        mm = tensor.matmul(
                            p_bc[:, ppar * 512:(ppar + 1) * 512],
                            lnv(par, cc)[:, tt4 * 128:(tt4 + 1) * 128],
                            wv_sb[:, cc * 512:(cc + 1) * 512],
                            start=(cc == 0), stop=(cc == 7))
                        if cc == 7:
                            s_p.inc(mm)
                VM_U = s_p.n

                @block.vector
                def _(vector, VM_U=VM_U, ppar=ppar, vu=vu):
                    vector.wait_ge(s_p.h, VM_U)
                    if vu >= 2:
                        vector.wait_ge(s_vw.h, 16 * vu)
                    vt = vtile[:, ppar * 520:(ppar + 1) * 520]
                    vector.tensor_copy(
                        vt.rearrange("p (h c) -> p h c", h=8)[:, :, 0:64],
                        p_bc[:, ppar * 512:(ppar + 1) * 512]
                        .rearrange("p (h c) -> p h c", h=8))
                    vector.drain()
                    s_v.inc(vector.memset(
                        vt.rearrange("p (h c) -> p h c", h=8)[:, :, 64:65],
                        1.0))
                VE_U = s_v.n
                ve_done.append(VE_U)

                @block.sync
                def _(sync, g_i=g_i, tt4=tt4, VE_U=VE_U, ppar=ppar):
                    sync.wait_ge(s_v.h, VE_U)
                    tt = g_i * 4 + tt4
                    s_vw.dma(sync.dma_start(
                        out=v_scr[tt * 128:(tt + 1) * 128, :],
                        in_=vtile[:, ppar * 520:(ppar + 1) * 520]))
                vunit[0] += 1
            if g_i + 2 < GRB:
                emit_x_load(g_i + 2)

        QKV_KQ = s_kq.n
        QKV_VW = s_vw.n
        V_QKV = s_v.n
        assert V_QKV == V0 + GRB * PERG, (V_QKV, V0 + GRB * PERG)

        if stage < 2:
            @block.gpsimd
            def _(g):
                g.wait_ge(s_kq.h, QKV_KQ)
                g.wait_ge(s_vw.h, QKV_VW)
                g.dma_start(out=xout[:, :], in_=xresi[:, :]).then_inc(s_g.h, 16)
                g.wait_ge(s_g.h, s_g.n + 16)
            return nc

        @block.scalar
        def _(scalar):
            scalar.wait_ge(s_v.h, s_v.n)
            s_s.inc(scalar.activation(out=nrow[0:1, 5:6], in_=nrow[0:1, 1:2],
                                      func=AF.Sqrt))
        S_CB = s_s.n

        @block.vector
        def _(vector):
            vector.wait_ge(s_s.h, S_CB)
            s_v.inc(vector.tensor_scalar(out=nrow[0:1, 5:6],
                                         in0=nrow[0:1, 5:6],
                                         scalar1=-1.0, scalar2=None,
                                         op0=ALU.mult))
        V_NEGC = s_v.n

        @block.sync
        def _(sync):
            sync.wait_ge(s_vw.h, QKV_VW)
            for tt in range(32):
                s_lv.dma(sync.dma_start(
                    out=v_sb[:, tt * 520:(tt + 1) * 520],
                    in_=v_scr[tt * 128:(tt + 1) * 128, :]))
        LV_ALL = s_lv.n

        head_last = {}
        exp_done = []
        pv_done = []
        tile_u = [0]

        def emit_kq_load(h, guard=None):
            @block.sync
            def _(sync, h=h, guard=guard):
                if h < 2:
                    sync.wait_ge(s_kq.h, QKV_KQ)
                else:
                    sync.wait_ge(s_p.h, guard)
                s_lk.dma(sync.dma_start(
                    out=kbuf[:, (h % 2) * TB:(h % 2 + 1) * TB],
                    in_=kq_scr[h * 128:(h + 1) * 128, :]))
                s_lk.dma(sync.dma_start(
                    out=qbuf[:, (h % 2) * TB:(h % 2 + 1) * TB],
                    in_=kq_scr[(NHL + h) * 128:(NHL + h + 1) * 128, :]))

        emit_kq_load(0)
        emit_kq_load(1)
        last_nrms = [None]
        prev_apc = {0: None, 1: None}
        prev_bcd = {0: None, 1: None}

        bq32 = sp2[0:1, 0:512]
        bqb = sp2[:, 512:768].bitcast(BF16)[0:1, 0:512]
        prev_bq = [None]
        for h in range(NHL):
            for qc in range(8):
                @block.scalar
                def _(scalar, h=h, qc=qc, pq0=pv_done[-1] if pv_done else None):
                    if h == 0 and qc == 0:
                        scalar.wait_ge(s_v.h, V_NEGC)
                    if qc == 0:
                        scalar.wait_ge(s_lk.h, 32 * min(h + 2, NHL))
                    s_s.inc(scalar.activation(
                        out=sp1[:, 0:512],
                        in_=qbuf[:, (h % 2) * TB + qc * 512:
                                 (h % 2) * TB + (qc + 1) * 512],
                        func=AF.Square))
                SQQ_U = s_s.n

                @block.tensor
                def _(tensor, SQQ_U=SQQ_U, pb=prev_bq[0]):
                    tensor.wait_ge(s_s.h, SQQ_U)
                    if pb is not None:
                        tensor.wait_ge(s_s.h, pb)
                    s_p.inc(tensor.matmul(p_n[0:1, :], onesf_col[:, 0:1],
                                          sp1[:, 0:512], start=True, stop=True))
                QN_U = s_p.n

                @block.scalar
                def _(scalar, QN_U=QN_U, pq0=pv_done[-1] if pv_done else None):
                    scalar.wait_ge(s_p.h, QN_U)
                    if pq0 is not None:
                        scalar.wait_ge(s_p.h, pq0)
                    scalar.activation(out=bq32, in_=p_n[0:1, :],
                                      func=AF.Sqrt)
                    scalar.drain()
                    s_s.inc(scalar.activation(out=bqb, in_=bq32,
                                              func=AF.Copy,
                                              scale=nrow[0:1, 5:6]))
                BQ_U = s_s.n
                prev_bq[0] = BQ_U

                for s in range(4 * qc + 4):
                    u = tile_u[0]
                    ppar = u % 2
                    diag = (s // 4 == qc)

                    @block.tensor
                    def _(tensor, h=h, qc=qc, s=s, u=u, ppar=ppar, BQ_U=BQ_U):
                        if u == 0:
                            tensor.wait_ge(s_lv.h, LV_ALL)
                            tensor.wait_ge(s_v.h, V_NEGC)
                        if qc == 0 and s == 0:
                            tensor.wait_ge(s_lk.h, 32 * min(h + 2, NHL))
                        if u >= 2:
                            tensor.wait_ge(s_s.h, exp_done[u - 2])
                        if s == 0:
                            tensor.wait_ge(s_s.h, BQ_U)
                        hb = (h % 2) * TB
                        tensor.matmul(
                            p_q[:, ppar * 512:(ppar + 1) * 512],
                            kbuf[:, hb + s * 128:hb + (s + 1) * 128],
                            qbuf[:, hb + qc * 512:hb + (qc + 1) * 512],
                            start=True, stop=False)
                        s_p.inc(tensor.matmul(
                            p_q[:, ppar * 512:(ppar + 1) * 512],
                            ones_rowb[0:1, :], bqb,
                            start=False, stop=True))
                    MMS_U = s_p.n

                    @block.scalar
                    def _(scalar, MMS_U=MMS_U, u=u, ppar=ppar):
                        scalar.wait_ge(s_p.h, MMS_U)
                        if u >= 2:
                            scalar.wait_ge(s_p.h, pv_done[u - 2])
                        s_s.inc(scalar.activation(
                            out=probs[:, ppar * 512:(ppar + 1) * 512],
                            in_=p_q[:, ppar * 512:(ppar + 1) * 512],
                            func=AF.Exp, scale=0.125))
                    EXP_U = s_s.n
                    exp_done.append(EXP_U)

                    MSK_U = None
                    if diag:
                        @block.vector
                        def _(vector, EXP_U=EXP_U, s=s, ppar=ppar):
                            vector.wait_ge(s_s.h, EXP_U)
                            s_v.inc(vector.tensor_tensor(
                                out=probs[:, ppar * 512:(ppar + 1) * 512],
                                in0=probs[:, ppar * 512:(ppar + 1) * 512],
                                in1=tri[:, (s % 4) * 512:(s % 4 + 1) * 512],
                                op=ALU.mult))
                        MSK_U = s_v.n

                    @block.tensor
                    def _(tensor, h=h, qc=qc, s=s, ppar=ppar, EXP_U=EXP_U,
                          MSK_U=MSK_U, pn=last_nrms[0]):
                        if MSK_U is not None:
                            tensor.wait_ge(s_v.h, MSK_U)
                        else:
                            tensor.wait_ge(s_s.h, EXP_U)
                        if s == 0 and pn is not None:
                            tensor.wait_ge(s_s.h, pn)
                        s_p.inc(tensor.matmul(
                            p_pv[:, :],
                            v_sb[:, s * 520 + h * 65:s * 520 + h * 65 + 65],
                            probs[:, ppar * 512:(ppar + 1) * 512],
                            start=(s == 0), stop=(s == 4 * qc + 3)))
                    pv_done.append(s_p.n)
                    tile_u[0] += 1

                npar = (h * 8 + qc) % 2

                @block.scalar
                def _(scalar, PV_QC=s_p.n, npar=npar, pa=prev_apc[npar],
                      pb=prev_bcd[npar]):
                    scalar.wait_ge(s_p.h, PV_QC)
                    if pa is not None:
                        scalar.wait_ge(s_v.h, pa)
                    if pb is not None:
                        scalar.wait_ge(s_p.h, pb)
                    dr = drow[0:1, npar * 512:(npar + 1) * 512]
                    scalar.activation(out=dr, in_=p_pv[64:65, :], func=AF.Ln)
                    scalar.drain()
                    scalar.activation(out=dr, in_=dr, func=AF.Exp, scale=-1.0)
                    s_s.inc(scalar.copy(
                        out=araw[0:64, npar * 512:(npar + 1) * 512],
                        in_=p_pv[0:64, :]))
                NRMS_U = s_s.n
                last_nrms[0] = NRMS_U

                @block.tensor
                def _(tensor, NRMS_U=NRMS_U, npar=npar, pa=prev_apc[npar]):
                    tensor.wait_ge(s_s.h, NRMS_U)
                    if pa is not None:
                        tensor.wait_ge(s_v.h, pa)
                    s_p.inc(tensor.matmul(
                        p_bc[0:64, npar * 512:(npar + 1) * 512],
                        ones_row[0:1, 0:64],
                        drow[0:1, npar * 512:(npar + 1) * 512],
                        start=True, stop=True))
                BCD_U = s_p.n
                head_last[h] = BCD_U
                prev_bcd[npar] = BCD_U

                @block.vector
                def _(vector, BCD_U=BCD_U, npar=npar, h=h, qc=qc):
                    vector.wait_ge(s_p.h, BCD_U)
                    s_v.inc(vector.tensor_tensor(
                        out=at_sb[(h % 2) * 64:(h % 2) * 64 + 64,
                                  (h // 2) * TB + qc * 512:
                                  (h // 2) * TB + (qc + 1) * 512],
                        in0=araw[0:64, npar * 512:(npar + 1) * 512],
                        in1=p_bc[0:64, npar * 512:(npar + 1) * 512],
                        op=ALU.mult))
                APC_U = s_v.n
                prev_apc[npar] = APC_U
            if h + 2 < NHL:
                emit_kq_load(h + 2, guard=head_last[h])
        AT_V = s_v.n

        if stage < 3:
            @block.gpsimd
            def _(g):
                g.wait_ge(s_v.h, AT_V)
                g.dma_start(out=xout[:, :], in_=xresi[:, :]).then_inc(s_g.h, 16)
                g.wait_ge(s_g.h, s_g.n + 16)
            return nc

        pe_done = []
        for tt in range(32):
            for eh in range(2):
                uu = tt * 2 + eh
                ppar = uu % 2

                @block.tensor
                def _(tensor, tt=tt, eh=eh, uu=uu, ppar=ppar):
                    if uu == 0:
                        tensor.wait_ge(s_v.h, AT_V)
                    if uu >= 2:
                        tensor.wait_ge(s_s.h, pe_done[uu - 2])
                    for cc in range(4):
                        mm = tensor.matmul(
                            p_q[:, ppar * 512:(ppar + 1) * 512],
                            at_sb[:, cc * TB + tt * 128:cc * TB + (tt + 1) * 128],
                            wp_sb[:, cc * 1024 + eh * 512:cc * 1024 + (eh + 1) * 512],
                            start=(cc == 0), stop=(cc == 3))
                        if cc == 3:
                            s_p.inc(mm)
                PM_U = s_p.n

                @block.scalar
                def _(scalar, PM_U=PM_U, ppar=ppar, uu=uu):
                    scalar.wait_ge(s_p.h, PM_U)
                    if uu >= 2:
                        scalar.wait_ge(s_pw.h, 16 * uu)
                    s_s.inc(scalar.copy(
                        out=prjev[:, ppar * 512:(ppar + 1) * 512],
                        in_=p_q[:, ppar * 512:(ppar + 1) * 512]))
                PE_U = s_s.n
                pe_done.append(PE_U)

                @block.sync
                def _(sync, tt=tt, eh=eh, PE_U=PE_U, ppar=ppar):
                    sync.wait_ge(s_s.h, PE_U)
                    s_pw.dma(sync.dma_start(
                        out=rs_in[tt * 128:(tt + 1) * 128,
                                  eh * 512:(eh + 1) * 512],
                        in_=prjev[:, ppar * 512:(ppar + 1) * 512]))
        PW_ALL = s_pw.n

        @block.gpsimd
        def _(g):
            g.wait_ge(s_pw.h, PW_ALL)
            s_cc.inc(g.collective_compute(
                "ReduceScatter", ALU.add,
                replica_groups=[[0, 1], [2, 3], [4, 5], [6, 7]],
                ins=[rs_in.ap().opt()], outs=[rs_out.ap().opt()]))
        CC_N = s_cc.n

        ores_done = {}

        def emit_res_load(tt, guard=None):
            @block.sync
            def _(sync, tt=tt, guard=guard):
                par = tt % 2
                if tt < 2:
                    sync.wait_ge(s_cc.h, CC_N)
                else:
                    sync.wait_ge(s_v.h, guard)
                    sync.wait_ge(s_st.h, 16 * (tt - 1))
                s_ro.dma(sync.dma_start(
                    out=rsld[:, par * 1024:(par + 1) * 1024],
                    in_=rs_out[tt * 128:(tt + 1) * 128, :]))
                s_ro.dma(sync.dma_start(
                    out=(xrld1 if par == 0 else xrld2)[:, 0:1024],
                    in_=xresi[tt * 128:(tt + 1) * 128, :]))

        emit_res_load(0)
        emit_res_load(1)
        for tt in range(16):
            par = tt % 2

            @block.vector
            def _(vector, tt=tt, par=par):
                vector.wait_ge(s_ro.h, 32 * min(tt + 2, 16))
                s_v.inc(vector.tensor_tensor(
                    out=(xrld1 if par == 0 else xrld2)[:, 0:1024],
                    in0=(xrld1 if par == 0 else xrld2)[:, 0:1024],
                    in1=rsld[:, par * 1024:(par + 1) * 1024], op=ALU.add))
            ores_done[tt] = s_v.n

            @block.sync
            def _(sync, tt=tt, par=par):
                sync.wait_ge(s_v.h, ores_done[tt])
                s_st.dma(sync.dma_start(
                    out=xout[tt * 128:(tt + 1) * 128, :],
                    in_=(xrld1 if par == 0 else xrld2)[:, 0:1024]))
            if tt + 2 < 16:
                emit_res_load(tt + 2, guard=ores_done[tt])

        @block.sync
        def _(sync):
            sync.wait_ge(s_st.h, s_st.n)

    return nc


def stage_l0_attn_inputs(x_full, inp):
    w_attn = np.asarray(inp["L0_attn_w_attn"], dtype=np.float32)
    w_proj = np.asarray(inp["L0_attn_w_proj"], dtype=np.float32)
    delta = np.asarray(inp["L0_attn_delta"], dtype=np.float32)
    x = np.asarray(x_full, dtype=np.float32)
    maps = []
    wpT = np.ascontiguousarray(w_proj.T)
    for core in range(CORES):
        b, hg = core // 2, core % 2
        heads = list(range(hg * 8, hg * 8 + 8))
        fsel = []
        for qk in range(2):
            base = qk * 1024
            for fc4 in range(4):
                h0 = heads[fc4 * 2]
                fsel.append(base + h0 * 64)
        blocks = []
        for fb in fsel:
            wslab = w_attn[fb:fb + 128, :]
            blk = wslab.reshape(128, 8, 128).transpose(2, 1, 0)
            blocks.append(blk.reshape(128, 8 * 128))
        wqk_h = np.ascontiguousarray(np.concatenate(blocks, axis=0)
                                     .reshape(8 * 128, 8 * 128))
        vrows = np.concatenate([w_attn[2048 + h * 64:2048 + (h + 1) * 64, :]
                                for h in heads], axis=0)
        wv_h = np.ascontiguousarray(vrows.T.reshape(8, 128, 512)
                                    .reshape(8 * 128, 512))
        prow = np.concatenate([wpT[h * 64:(h + 1) * 64, :] for h in heads],
                              axis=0)
        wp_h = np.ascontiguousarray(prow.reshape(4 * 128, 1024))
        dl = np.zeros((128, 4), np.float32)
        for kc in range(4):
            dl[0:64, kc] = delta[heads[kc * 2]]
            dl[64:128, kc] = delta[heads[kc * 2 + 1]]
        xfull_c = np.ascontiguousarray(x[b].T)
        xres_c = np.ascontiguousarray(x[b, hg * 2048:(hg + 1) * 2048, :])
        maps.append(dict(xfull=xfull_c, xresi=xres_c, wqk=wqk_h, wvh=wv_h,
                         wph=wp_h, dlt=dl))
    return maps


def unshard_l0_out(shards):
    out = np.zeros((B, T, C), dtype=np.float32)
    for core in range(CORES):
        b, hg = core // 2, core % 2
        out[b, hg * 2048:(hg + 1) * 2048, :] = np.asarray(shards[core])
    return out



ROWW = 1040
LCAP = 640
GLON = CORES * LCAP
NTIL = TOKC // 128
RTIL = GLON // 128
GSZ = 2
NGRP = RTIL // GSZ
BIGF = 100000.0
CAPN = 1025.0
BIT = 52
HID = 4096


def build_l0_ffn():
    nc = bass.Bass("TRN2", target_bir_lowering=False, debug=False,
                   dynamic_dma_scratch_size=28672)
    xin = nc.dram_tensor("xin", [TOKC, C], F32, kind="ExternalInput").ap()
    rwt = nc.dram_tensor("rwt", [C, 8], F32, kind="ExternalInput").ap()
    w1e = nc.dram_tensor("w1e", [C, HID], F32, kind="ExternalInput").ap()
    w2e = nc.dram_tensor("w2e", [HID, C], F32, kind="ExternalInput").ap()
    b1e = nc.dram_tensor("b1e", [128, 32], F32, kind="ExternalInput").ap()
    b2e = nc.dram_tensor("b2e", [1, C], F32, kind="ExternalInput").ap()
    onehot = nc.dram_tensor("onehot", [128, 8], F32, kind="ExternalInput").ap()
    xout = nc.dram_tensor("xout", [TOKC, C], F32, kind="ExternalOutput").ap()
    sc_loc = nc.dram_tensor("sc_loc", [8, TOKC], F32)
    sc_g = nc.dram_tensor("sc_g", [CORES, 8, TOKC], F32)
    xnd = nc.dram_tensor("xnd", [TOKC, ROWW], BF16)
    glo = nc.dram_tensor("glo", [GLON, ROWW], BF16)
    gre = nc.dram_tensor("gre", [GLON, ROWW], BF16)
    yst = nc.dram_tensor("yst", [GLON, C], BF16)
    yret = nc.dram_tensor("yret", [GLON, C], BF16)

    with ExitStack() as ctx:
        e = lambda t: ctx.enter_context(t)
        arena = e(nc.sbuf_tensor("arena", [128, 81920], BF16))
        w1sb = arena[:, 0:32768]
        xnaug = arena[:, 32768:49408]
        xtv = arena[:, 49408:53504].bitcast(F32)
        sqsc = arena[:, 53504:55552].bitcast(F32)
        sstv = arena[:, 55552:59648].bitcast(F32)
        mkA = arena[:, 59648:63744].bitcast(F32)
        w2sb = arena[:, 32768:65536]
        hsb = arena[:, 65536:73728]
        mkB = arena[:, 65536:69632].bitcast(F32)
        b2f = arena[:, 69632:71680].bitcast(F32)
        rsf = arena[:, 65536:73728].bitcast(F32)
        scgv = arena[:, 49408:51456].bitcast(F32)
        dumb = arena[:, 51456:52480]
        sltab = arena[:, 49408:50688].bitcast(F32)
        invf = arena[:, 50688:51968].bitcast(F32)[0:1, 0:640]
        oh1 = sqsc[:, 0:640]
        invu = arena[:, 54784:56064].bitcast(U32)[0:1, 0:640]
        wld = arena[:, 73728:81920].bitcast(F32)
        yrall = arena[:, 0:40960]

        xfm = e(nc.sbuf_tensor("xfm", [128, 2048], BF16))
        gt = e(nc.sbuf_tensor("gt", [128, 2 * ROWW], BF16))
        ysb = e(nc.sbuf_tensor("ysb", [128, 2048], BF16))
        b2b = e(nc.sbuf_tensor("b2b", [1, C], BF16))
        rwtf = e(nc.sbuf_tensor("rwtf", [128, 64], F32))
        rwtsb = e(nc.sbuf_tensor("rwtsb", [128, 64], BF16))
        b1sb = e(nc.sbuf_tensor("b1sb", [128, 32], F32))
        idn = e(nc.sbuf_tensor("idn", [128, 128], I32))
        identf = e(nc.sbuf_tensor("identf", [128, 128], F32))
        identb = e(nc.sbuf_tensor("identb", [128, 128], BF16))
        sel = e(nc.sbuf_tensor("sel", [128, 32], F32))
        selT = e(nc.sbuf_tensor("selT", [32, 128], F32))
        selsc = e(nc.sbuf_tensor("selsc", [128, 128], F32))
        sco = e(nc.sbuf_tensor("sco", [128, 8], F32))
        sco2 = e(nc.sbuf_tensor("sco2", [128, 8], F32))
        tmp8 = e(nc.sbuf_tensor("tmp8", [128, 8], F32))
        ohf = e(nc.sbuf_tensor("ohf", [128, 8], F32))
        oneb = e(nc.sbuf_tensor("oneb", [128, 8], BF16))
        tkm = e(nc.sbuf_tensor("tkm", [128, 16], F32))
        invu8 = e(nc.sbuf_tensor("invu8", [128, 40], U32))
        stc = e(nc.sbuf_tensor("stc", [128, 12], F32))
        bis = e(nc.sbuf_tensor("bis", [32, 8], F32))
        midc = e(nc.sbuf_tensor("midc", [128, 1], F32))
        thrc = e(nc.sbuf_tensor("thrc", [128, 1], F32))
        accol = e(nc.sbuf_tensor("accol", [128, 1], F32))
        throw = e(nc.sbuf_tensor("throw", [1, 32], F32))
        ones512 = e(nc.sbuf_tensor("ones512", [1, 512], F32))
        onesbc = e(nc.sbuf_tensor("onesbc", [1, 128], BF16))
        gcolg = e(nc.sbuf_tensor("gcolg", [128, 2], F32))
        slotc = e(nc.sbuf_tensor("slotc", [128, 5], F32))
        ohtb = e(nc.sbuf_tensor("ohtb", [128, 1280], BF16))
        offrow = e(nc.sbuf_tensor("offrow", [1, 256], F32))
        cns = e(nc.sbuf_tensor("cns", [128, 1], F32))

        p_mm1 = e(nc.psum_tensor("p_mm1", [128, 1024], F32))
        p_mm2 = e(nc.psum_tensor("p_mm2", [128, 1024], F32))
        p_tr = e(nc.psum_tensor("p_tr", [128, 256], BF16))
        p_mix = e(nc.psum_tensor("p_mix", [128, 512], F32))
        p_c = e(nc.psum_tensor("p_c", [128, 1], F32))

        s_ld = Sem(e(nc.semaphore("s_ld")))
        s_w1 = Sem(e(nc.semaphore("s_w1")))
        s_w2 = Sem(e(nc.semaphore("s_w2")))
        s_ys = Sem(e(nc.semaphore("s_ys")))
        s_st = Sem(e(nc.semaphore("s_st")))
        s_v = Sem(e(nc.semaphore("s_v")))
        s_s = Sem(e(nc.semaphore("s_s")))
        s_p = Sem(e(nc.semaphore("s_p")))
        s_io = Sem(e(nc.semaphore("s_io")))
        s_sc = Sem(e(nc.semaphore("s_sc")))
        s_cc = Sem(e(nc.semaphore("s_cc")))

        nc.const_aps.register(F32, LNEPS, cns[:, 0:1])
        block = e(nc.Block())

        @block.gpsimd
        def _(g):
            s_io.inc(g.iota(idn[:, 0:32], pattern=[[-4, 32]], base=0,
                            channel_multiplier=1))
        IO_A = s_io.n

        @block.vector
        def _(vector):
            s_v.inc(vector.memset(ones512[0:1, :], 1.0))
            s_v.inc(vector.memset(onesbc[0:1, :], 1.0))
            s_v.inc(vector.memset(cns[:, 0:1], LNEPS))
            vector.wait_ge(s_io.h, IO_A)
            vector.tensor_copy(sel[:, 0:32], idn[:, 0:32])
            vector.drain()
            vector.tensor_scalar(out=selsc[:, 0:32], in0=sel[:, 0:32],
                                 scalar1=-0.5, scalar2=None, op0=ALU.is_ge)
            vector.tensor_scalar(out=sel[:, 0:32], in0=sel[:, 0:32],
                                 scalar1=3.5, scalar2=None, op0=ALU.is_le)
            vector.drain()
            s_v.inc(vector.tensor_tensor(out=sel[:, 0:32], in0=sel[:, 0:32],
                                         in1=selsc[:, 0:32], op=ALU.mult))
        V_SELA = s_v.n

        @block.gpsimd
        def _(g):
            g.wait_ge(s_v.h, V_SELA)
            s_io.inc(g.iota(idn[0:32, 0:128], pattern=[[1, 128]], base=0,
                            channel_multiplier=-4))
        IO_B = s_io.n

        @block.vector
        def _(vector):
            vector.wait_ge(s_io.h, IO_B)
            vector.tensor_copy(selT[0:32, :], idn[0:32, 0:128])
            vector.drain()
            vector.tensor_scalar(out=selsc[0:32, 0:128], in0=selT[0:32, :],
                                 scalar1=-0.5, scalar2=None, op0=ALU.is_ge)
            vector.tensor_scalar(out=selT[0:32, :], in0=selT[0:32, :],
                                 scalar1=3.5, scalar2=None, op0=ALU.is_le)
            vector.drain()
            s_v.inc(vector.tensor_tensor(out=selT[0:32, :], in0=selT[0:32, :],
                                         in1=selsc[0:32, 0:128], op=ALU.mult))
        V_SELB = s_v.n

        @block.gpsimd
        def _(g):
            g.wait_ge(s_v.h, V_SELB)
            s_io.inc(g.iota(idn[:, 0:128], pattern=[[-1, 128]], base=0,
                            channel_multiplier=1))
        IO_C = s_io.n

        @block.vector
        def _(vector):
            vector.wait_ge(s_io.h, IO_C)
            vector.tensor_copy(identf[:, :], idn[:, 0:128])
            vector.drain()
            vector.tensor_scalar(out=identf[:, :], in0=identf[:, :],
                                 scalar1=0.0, scalar2=None, op0=ALU.is_equal)
            vector.drain()
            s_v.inc(vector.tensor_copy(identb[:, :], identf[:, :]))
        V_IDENT = s_v.n

        @block.sync
        def _(sync):
            for d in range(8):
                s_ld.dma(sync.dma_start(out=rwtf[:, d * 8:(d + 1) * 8],
                                        in_=rwt[d * 128:(d + 1) * 128, :]))
            s_ld.dma(sync.dma_start(out=ohf[:, :], in_=onehot[:, :]))
            s_ld.dma(sync.dma_start(out=b1sb[:, :], in_=b1e[:, :]))
            s_ld.dma(sync.dma_start(out=b2f[0:1, 0:1024], in_=b2e[0:1, :]))
        LD_PARAM = s_ld.n

        @block.vector
        def _(vector):
            vector.wait_ge(s_ld.h, LD_PARAM)
            vector.tensor_copy(rwtsb[:, :], rwtf[:, :])
            vector.tensor_copy(oneb[:, :], ohf[:, :])
            s_v.inc(vector.tensor_copy(b2b[0:1, :], b2f[0:1, 0:1024]))
        V_PCAST = s_v.n

        ln_apply = []
        for i in range(NTIL):
            par = i % 2
            xtp = xtv[:, par * 1024:(par + 1) * 1024]

            @block.sync
            def _(sync, i=i, par=par):
                if i >= 2:
                    sync.wait_ge(s_v.h, ln_apply[i - 2])
                s_ld.dma(sync.dma_start(out=xtv[:, par * 1024:(par + 1) * 1024],
                                        in_=xin[i * 128:(i + 1) * 128, :]))
            LDX_I = s_ld.n

            @block.vector
            def _(vector, LDX_I=LDX_I, xtp=xtp):
                vector.wait_ge(s_ld.h, LDX_I)
                s_v.inc(vector.tensor_reduce(out=stc[:, 0:1], in_=xtp,
                                             axis=AX.X, op=ALU.add))
            V_R1 = s_v.n

            @block.scalar
            def _(scalar, V_R1=V_R1, xtp=xtp):
                scalar.wait_ge(s_v.h, V_R1)
                s_s.inc(scalar.activation(out=sqsc[:, :], in_=xtp,
                                          func=AF.Square))
            S_SQ = s_s.n

            @block.vector
            def _(vector, S_SQ=S_SQ):
                vector.wait_ge(s_s.h, S_SQ)
                vector.tensor_reduce(out=stc[:, 1:2], in_=sqsc[:, :],
                                     axis=AX.X, op=ALU.add)
                vector.drain()
                vector.tensor_scalar(out=stc[:, 2:3], in0=stc[:, 0:1],
                                     scalar1=1.0 / 1024.0, scalar2=None,
                                     op0=ALU.mult)
                vector.tensor_scalar(out=stc[:, 3:4], in0=stc[:, 1:2],
                                     scalar1=1.0 / 1024.0, scalar2=None,
                                     op0=ALU.mult)
                vector.drain()
                vector.tensor_tensor(out=stc[:, 4:5], in0=stc[:, 2:3],
                                     in1=stc[:, 2:3], op=ALU.mult)
                vector.drain()
                s_v.inc(vector.tensor_tensor(out=stc[:, 5:6], in0=stc[:, 3:4],
                                             in1=stc[:, 4:5], op=ALU.subtract))
            V_VAR = s_v.n

            @block.scalar
            def _(scalar, V_VAR=V_VAR):
                scalar.wait_ge(s_v.h, V_VAR)
                scalar.activation(out=stc[:, 6:7], in_=stc[:, 5:6],
                                  func=AF.Ln, bias=LNEPS)
                scalar.drain()
                s_s.inc(scalar.activation(out=stc[:, 7:8], in_=stc[:, 6:7],
                                          func=AF.Exp, scale=-0.5))
            S_RSTD = s_s.n

            @block.vector
            def _(vector, S_RSTD=S_RSTD, i=i, xtp=xtp):
                vector.wait_ge(s_s.h, S_RSTD)
                s_v.inc(vector.tensor_scalar(
                    out=xnaug[:, i * ROWW:i * ROWW + 1024], in0=xtp,
                    scalar1=stc[:, 2:3], scalar2=stc[:, 7:8],
                    op0=ALU.subtract, op1=ALU.mult))
            ln_apply.append(s_v.n)
        V_LN15 = s_v.n

        w1cast = []
        @block.scalar
        def _(scalar):
            for t in range(32):
                scalar.wait_ge(s_w1.h, 16 * (t + 1))
                dc, hb = t // 4, t % 4
                s_s.inc(scalar.activation(
                    out=w1sb[:, dc * 4096 + hb * 1024:
                             dc * 4096 + (hb + 1) * 1024],
                    in_=wld[:, (t % 4) * 1024:(t % 4 + 1) * 1024],
                    func=AF.Copy))
                w1cast.append(s_s.n)
        S_W1CAST = s_s.n

        @block.sync
        def _(sync):
            for t in range(32):
                if t >= 4:
                    sync.wait_ge(s_s.h, w1cast[t - 4])
                dc, hb = t // 4, t % 4
                s_w1.dma(sync.dma_start(
                    out=wld[:, (t % 4) * 1024:(t % 4 + 1) * 1024],
                    in_=w1e[dc * 128:(dc + 1) * 128,
                            hb * 1024:(hb + 1) * 1024]))

        sst = sstv[0:8, 0:2048]
        sc_mm = []
        sst_ev = []
        xnd_st = []
        trv_prev = None
        for i in range(NTIL):
            trv = []
            for d in range(8):
                dp = d % 2

                @block.tensor
                def _(tensor, i=i, d=d, dp=dp, tq=trv_prev):
                    if i == 0 and d == 0:
                        tensor.wait_ge(s_v.h, V_IDENT)
                    tensor.wait_ge(s_v.h, ln_apply[i])
                    if d >= 2:
                        tensor.wait_ge(s_s.h, trv[d - 2])
                    elif i >= 1:
                        tensor.wait_ge(s_s.h, tq[6 + d])
                    s_p.inc(tensor.transpose(
                        p_tr[:, dp * 128:(dp + 1) * 128],
                        xnaug[:, i * ROWW + d * 128:i * ROWW + (d + 1) * 128],
                        identb[:, :]))
                TR_U = s_p.n

                @block.scalar
                def _(scalar, TR_U=TR_U, d=d, dp=dp, i=i):
                    scalar.wait_ge(s_p.h, TR_U)
                    if i >= 1 and d == 0:
                        scalar.wait_ge(s_p.h, sc_mm[i - 1])
                    s_s.inc(scalar.activation(
                        out=xfm[:, d * 128:(d + 1) * 128],
                        in_=p_tr[:, dp * 128:(dp + 1) * 128], func=AF.Copy))
                trv.append(s_s.n)
            trv_prev = trv

            @block.tensor
            def _(tensor, i=i, trv=trv):
                tensor.wait_ge(s_s.h, trv[-1])
                if i == 0:
                    tensor.wait_ge(s_v.h, V_PCAST)
                if i >= 1:
                    tensor.wait_ge(s_v.h, sst_ev[i - 1])
                for d in range(8):
                    mm = tensor.matmul(p_mix[:, 0:8],
                                       xfm[:, d * 128:(d + 1) * 128],
                                       rwtsb[:, d * 8:(d + 1) * 8],
                                       start=(d == 0), stop=(d == 7))
                    if d == 7:
                        s_p.inc(mm)
            sc_mm.append(s_p.n)
            MM_SC = s_p.n

            @block.vector
            def _(vector, MM_SC=MM_SC):
                vector.wait_ge(s_p.h, MM_SC)
                vector.reduce_max(out=stc[:, 8:9], in_=p_mix[:, 0:8],
                                  axis=AX.X)
                vector.drain()
                s_v.inc(vector.tensor_scalar(out=stc[:, 9:10],
                                             in0=stc[:, 8:9], scalar1=-1.0,
                                             scalar2=None, op0=ALU.mult))
            V_NMX = s_v.n

            @block.scalar
            def _(scalar, V_NMX=V_NMX):
                scalar.wait_ge(s_v.h, V_NMX)
                s_s.inc(scalar.activation(out=sco[:, :], in_=p_mix[:, 0:8],
                                          func=AF.Exp, bias=stc[:, 9:10],
                                          scale=1.0))
            S_EXP = s_s.n

            @block.vector
            def _(vector, S_EXP=S_EXP, i=i):
                vector.wait_ge(s_s.h, S_EXP)
                vector.tensor_reduce(out=stc[:, 10:11], in_=sco[:, :],
                                     axis=AX.X, op=ALU.add)
                vector.drain()
                vector.reciprocal(out=stc[:, 11:12], in_=stc[:, 10:11])
                vector.drain()
                vector.tensor_scalar(
                    out=xnaug[:, i * ROWW + 1024:i * ROWW + 1032],
                    in0=sco[:, :], scalar1=stc[:, 11:12], scalar2=None,
                    op0=ALU.mult)
                s_v.inc(vector.tensor_scalar(
                    out=sco2[:, :], in0=sco[:, :], scalar1=stc[:, 11:12],
                    scalar2=None, op0=ALU.mult))
            V_SCN = s_v.n

            @block.sync
            def _(sync, V_SCN=V_SCN, i=i):
                sync.wait_ge(s_v.h, V_SCN)
                s_ld.dma(sync.dma_start(out=xnd[i * 128:(i + 1) * 128, :],
                                        in_=xnaug[:, i * ROWW:(i + 1) * ROWW]))
            xnd_st.append(s_ld.n)

            @block.tensor
            def _(tensor, V_SCN=V_SCN):
                tensor.wait_ge(s_v.h, V_SCN)
                s_p.inc(tensor.matmul(p_mix[0:8, 0:128], sco2[:, :],
                                      identf[:, :], start=True, stop=True))
            TR_SC = s_p.n

            @block.vector
            def _(vector, TR_SC=TR_SC, i=i):
                vector.wait_ge(s_p.h, TR_SC)
                s_v.inc(vector.tensor_copy(sst[:, i * 128:(i + 1) * 128],
                                           p_mix[0:8, 0:128]))
            sst_ev.append(s_v.n)
        V_SST = s_v.n
        XND_ALL = xnd_st[-1]

        @block.sync
        def _(sync):
            sync.wait_ge(s_v.h, V_SST)
            s_ld.dma(sync.dma_start(out=sc_loc[:, :], in_=sst[:, :]))
        LD_SCLOC = s_ld.n

        @block.gpsimd
        def _(g):
            g.wait_ge(s_ld.h, LD_SCLOC)
            s_cc.inc(g.collective_compute(
                "AllGather", ALU.bypass,
                replica_groups=[list(range(CORES))],
                ins=[sc_loc.ap().opt()], outs=[sc_g.ap().opt()]))
        CC_AG = s_cc.n

        @block.sync
        def _(sync):
            sync.wait_ge(s_cc.h, CC_AG)
            sync.wait_ge(s_v.h, V_LN15)
            for b in range(4):
                for ee in range(8):
                    for q in range(4):
                        p = (b * 8 + ee) * 4 + q
                        s_ld.dma(sync.dma_start(
                            out=scgv[p:p + 1, 0:1024],
                            in_=sc_g[2 * q:2 * q + 2, ee:ee + 1,
                                     b * 512:(b + 1) * 512]))
        LD_SCG = s_ld.n

        @block.vector
        def _(vector):
            vector.wait_ge(s_ld.h, LD_SCG)
            vector.memset(bis[:, 0:1], 0.0)
            s_v.inc(vector.memset(bis[:, 1:2], 1.0))
        V_BIS0 = s_v.n

        for it in range(BIT):
            @block.vector
            def _(vector, it=it):
                if it == 0:
                    vector.wait_ge(s_v.h, V_BIS0)
                vector.tensor_tensor(out=bis[:, 2:3], in0=bis[:, 0:1],
                                     in1=bis[:, 1:2], op=ALU.add)
                vector.drain()
                s_v.inc(vector.tensor_scalar(out=bis[:, 2:3], in0=bis[:, 2:3],
                                             scalar1=0.5, scalar2=None,
                                             op0=ALU.mult))
            V_MID = s_v.n

            @block.tensor
            def _(tensor, V_MID=V_MID):
                tensor.wait_ge(s_v.h, V_MID)
                s_p.inc(tensor.matmul(p_c[:, 0:1], selT[0:32, :],
                                      bis[0:32, 2:3], start=True, stop=True))
            P_BC = s_p.n

            @block.vector
            def _(vector, P_BC=P_BC):
                vector.wait_ge(s_p.h, P_BC)
                vector.tensor_copy(midc[:, 0:1], p_c[:, 0:1])
                vector.drain()
                s_v.inc(vector.tensor_scalar(
                    out=dumb[:, :], in0=scgv[:, 0:1024], scalar1=midc[:, 0:1],
                    scalar2=0.0, op0=ALU.is_ge, op1=ALU.add,
                    accum_out=accol[:, 0:1]))
            V_CNT = s_v.n

            @block.tensor
            def _(tensor, V_CNT=V_CNT):
                tensor.wait_ge(s_v.h, V_CNT)
                s_p.inc(tensor.matmul(p_c[0:32, 0:1], sel[:, 0:32],
                                      accol[:, 0:1], start=True, stop=True))
            P_CNT = s_p.n

            @block.vector
            def _(vector, P_CNT=P_CNT):
                vector.wait_ge(s_p.h, P_CNT)
                vector.tensor_copy(bis[:, 3:4], p_c[0:32, 0:1])
                vector.drain()
                vector.tensor_scalar(out=bis[:, 4:5], in0=bis[:, 3:4],
                                     scalar1=CAPN, scalar2=None, op0=ALU.is_ge)
                vector.drain()
                vector.tensor_tensor(out=bis[:, 5:6], in0=bis[:, 2:3],
                                     in1=bis[:, 0:1], op=ALU.subtract)
                vector.drain()
                vector.tensor_tensor(out=bis[:, 5:6], in0=bis[:, 4:5],
                                     in1=bis[:, 5:6], op=ALU.mult)
                vector.drain()
                vector.tensor_tensor(out=bis[:, 0:1], in0=bis[:, 0:1],
                                     in1=bis[:, 5:6], op=ALU.add)
                vector.drain()
                vector.tensor_tensor(out=bis[:, 6:7], in0=bis[:, 1:2],
                                     in1=bis[:, 2:3], op=ALU.subtract)
                vector.drain()
                vector.tensor_tensor(out=bis[:, 6:7], in0=bis[:, 4:5],
                                     in1=bis[:, 6:7], op=ALU.mult)
                vector.drain()
                s_v.inc(vector.tensor_tensor(out=bis[:, 1:2], in0=bis[:, 2:3],
                                             in1=bis[:, 6:7], op=ALU.add))
        V_BISD = s_v.n

        @block.tensor
        def _(tensor):
            tensor.wait_ge(s_v.h, V_BISD)
            s_p.inc(tensor.matmul(p_c[:, 0:1], selT[0:32, :], bis[0:32, 0:1],
                                  start=True, stop=True))
        P_THRB = s_p.n

        @block.vector
        def _(vector):
            vector.wait_ge(s_p.h, P_THRB)
            s_v.inc(vector.tensor_copy(thrc[:, 0:1], p_c[:, 0:1]))
        V_THRC = s_v.n

        @block.tensor
        def _(tensor):
            tensor.wait_ge(s_v.h, V_THRC)
            s_p.inc(tensor.matmul(p_mix[0:1, 0:32], thrc[:, 0:1],
                                  sel[:, 0:32], start=True, stop=True))
        P_THRW = s_p.n

        @block.vector
        def _(vector):
            vector.wait_ge(s_p.h, P_THRW)
            s_v.inc(vector.tensor_scalar(out=throw[0:1, :],
                                         in0=p_mix[0:1, 0:32], scalar1=0.25,
                                         scalar2=None, op0=ALU.mult))
        V_THROW = s_v.n

        msk_ev = []
        for b in range(4):
            @block.tensor
            def _(tensor, b=b):
                tensor.wait_ge(s_v.h, V_THROW)
                if b >= 1:
                    tensor.wait_ge(s_v.h, msk_ev[b - 1])
                s_p.inc(tensor.matmul(p_mix[0:8, 0:512],
                                      throw[0:1, b * 8:(b + 1) * 8],
                                      ones512[0:1, :], start=True, stop=True))
            P_TM = s_p.n

            @block.vector
            def _(vector, P_TM=P_TM, b=b):
                vector.wait_ge(s_p.h, P_TM)
                vector.tensor_tensor(out=mkA[0:8, b * 512:(b + 1) * 512],
                                     in0=sst[:, b * 512:(b + 1) * 512],
                                     in1=p_mix[0:8, 0:512], op=ALU.is_ge)
                vector.drain()
                s_v.inc(vector.tensor_tensor(
                    out=sst[:, b * 512:(b + 1) * 512],
                    in0=sst[:, b * 512:(b + 1) * 512],
                    in1=p_mix[0:8, 0:512], op=ALU.is_ge))
            msk_ev.append(s_v.n)

        @block.vector
        def _(vector):
            src, dst = mkA, mkB
            for st in range(11):
                k = 1 << st
                vector.tensor_copy(dst[0:8, 0:k], src[0:8, 0:k])
                vector.tensor_tensor(out=dst[0:8, k:2048],
                                     in0=src[0:8, k:2048],
                                     in1=src[0:8, 0:2048 - k], op=ALU.add)
                vector.drain()
                src, dst = dst, src
            vector.tensor_scalar(out=mkA[0:8, :], in0=mkB[0:8, :],
                                 scalar1=-1.0, scalar2=None, op0=ALU.add)
            vector.drain()
            vector.tensor_tensor(out=mkA[0:8, :], in0=mkA[0:8, :],
                                 in1=sst[:, :], op=ALU.mult)
            vector.drain()
            vector.tensor_scalar(out=mkB[0:8, :], in0=sst[:, :],
                                 scalar1=-BIGF, scalar2=None, op0=ALU.mult)
            vector.drain()
            vector.tensor_scalar(out=mkA[0:8, :], in0=mkA[0:8, :],
                                 scalar1=BIGF, scalar2=None, op0=ALU.add)
            vector.drain()
            vector.tensor_tensor(out=mkA[0:8, :], in0=mkA[0:8, :],
                                 in1=mkB[0:8, :], op=ALU.add)
            vector.drain()
            vector.tensor_scalar(out=mkB[0:8, :], in0=mkA[0:8, :],
                                 scalar1=639.5, scalar2=None, op0=ALU.is_ge)
            vector.drain()
            s_v.inc(vector.scalar_tensor_tensor(
                out=mkA[0:8, :], in0=mkB[0:8, :], scalar=BIGF,
                in1=mkA[0:8, :], op0=ALU.mult, op1=ALU.add))
        V_OFFS = s_v.n

        offc_ev = []
        for cch in range(NTIL):
            @block.tensor
            def _(tensor, cch=cch):
                tensor.wait_ge(s_v.h, V_OFFS)
                if cch >= 1:
                    tensor.wait_ge(s_v.h, offc_ev[cch - 1])
                s_p.inc(tensor.matmul(p_mix[:, 0:8],
                                      mkA[0:8, cch * 128:(cch + 1) * 128],
                                      identf[0:8, 0:8], start=True,
                                      stop=True))
            P_OC = s_p.n

            @block.vector
            def _(vector, P_OC=P_OC, cch=cch):
                vector.wait_ge(s_p.h, P_OC)
                s_v.inc(vector.tensor_copy(selsc[:, cch * 8:(cch + 1) * 8],
                                           p_mix[:, 0:8]))
            offc_ev.append(s_v.n)
        V_OFFC = s_v.n

        @block.gpsimd
        def _(g):
            g.wait_ge(s_v.h, V_OFFC)
            s_io.inc(g.iota(sltab[:, 0:640], pattern=[[1, 640]], base=0,
                            channel_multiplier=0,
                            allow_small_or_imprecise_dtypes=True))
            s_io.inc(g.iota(tkm[:, 0:16], pattern=[[128, 16]], base=1,
                            channel_multiplier=1,
                            allow_small_or_imprecise_dtypes=True))
        IO_D = s_io.n

        inv_rsh = []
        oh_ev = []
        inv_done = []
        for ee in range(8):
            for cch in range(NTIL):
                u = ee * NTIL + cch
                ohap = oh1 if u % 2 == 0 else xfm.bitcast(F32)[:, 0:640]

                @block.vector
                def _(vector, u=u, ee=ee, cch=cch, ohap=ohap):
                    if u == 0:
                        vector.wait_ge(s_io.h, IO_D)
                    if u >= 2:
                        vector.wait_ge(s_p.h, oh_ev[u - 2])
                    s_v.inc(vector.tensor_scalar(
                        out=ohap, in0=sltab[:, 0:640],
                        scalar1=selsc[:, cch * 8 + ee:cch * 8 + ee + 1],
                        scalar2=None, op0=ALU.is_equal))
                V_OH = s_v.n

                @block.tensor
                def _(tensor, V_OH=V_OH, cch=cch, ohap=ohap, ee=ee, u=u):
                    tensor.wait_ge(s_v.h, V_OH)
                    if u == 0:
                        tensor.wait_ge(s_s.h, s_s.n)
                    tensor.matmul(p_mm1[0:1, 0:512], tkm[:, cch:cch + 1],
                                  ohap[:, 0:512], start=(cch == 0),
                                  stop=(cch == 15))
                    s_p.inc(tensor.matmul(p_mm1[0:1, 512:640],
                                          tkm[:, cch:cch + 1],
                                          ohap[:, 512:640],
                                          start=(cch == 0), stop=(cch == 15)))
                oh_ev.append(s_p.n)
            P_INV = s_p.n

            @block.vector
            def _(vector, P_INV=P_INV, ee=ee):
                vector.wait_ge(s_p.h, P_INV)
                if ee >= 1:
                    vector.wait_ge(s_ld.h, inv_rsh[ee - 1])
                vector.tensor_copy(invf[0:1, 0:640], p_mm1[0:1, 0:640])
                vector.drain()
                vector.tensor_scalar(out=invf[0:1, 0:640],
                                     in0=invf[0:1, 0:640], scalar1=-1.0,
                                     scalar2=0.0, op0=ALU.add, op1=ALU.max)
                vector.drain()
                s_v.inc(vector.tensor_copy(invu[0:1, 0:640],
                                           invf[0:1, 0:640]))
            V_INV = s_v.n
            inv_done.append(V_INV)

            @block.sync
            def _(sync, V_INV=V_INV, ee=ee):
                sync.wait_ge(s_v.h, V_INV)
                for st in range(5):
                    s_ld.dma(sync.dma_start(
                        out=invu8[:, ee * 5 + st:ee * 5 + st + 1],
                        in_=invu[0:1, st * 128:(st + 1) * 128]))
            inv_rsh.append(s_ld.n)
        INV_ALL = inv_rsh[-1]

        @block.gpsimd
        def _(g):
            g.wait_ge(s_ld.h, XND_ALL)
            for k in range(RTIL):
                ee, st = k // 5, k % 5
                kp = k % 2
                g.wait_ge(s_ld.h, inv_rsh[ee])
                if k >= 2:
                    g.wait_ge(s_sc.h, 16 * (2 * (k - 2) + 2))
                s_sc.dma(g.indirect_dma_start(
                    out=gt[:, kp * ROWW:(kp + 1) * ROWW],
                    out_offset=None, in_=xnd[:, :],
                    in_offset=bass.IndirectOffsetOnAxis(
                        ap=invu8[:, ee * 5 + st:ee * 5 + st + 1], axis=0),
                    bounds_check=TOKC - 1, oob_is_err=False))
                g.wait_ge(s_sc.h, 16 * (2 * k + 1))
                s_sc.dma(g.dma_start(
                    out=glo[k * 128:(k + 1) * 128, :],
                    in_=gt[:, kp * ROWW:(kp + 1) * ROWW]))
        SC_ALL = s_sc.n

        @block.gpsimd
        def _(g):
            g.wait_ge(s_sc.h, SC_ALL)
            from concourse.aot_env import allow_ici_all_to_all
            with allow_ici_all_to_all():
                s_cc.inc(g.collective_compute(
                    "AllToAll", ALU.bypass,
                    replica_groups=[list(range(CORES))],
                    ins=[glo.ap().opt()], outs=[gre.ap().opt()]))
        CC_A2A = s_cc.n

        w2cast = []
        @block.vector
        def _(vector):
            vector.wait_ge(s_sc.h, SC_ALL)
            vector.wait_ge(s_ld.h, INV_ALL)
            for t in range(32):
                vector.wait_ge(s_w2.h, 16 * (t + 1))
                s_v.inc(vector.tensor_copy(
                    w2sb[:, t * 1024:(t + 1) * 1024],
                    wld[:, (t % 2) * 1024:(t % 2 + 1) * 1024]))
                w2cast.append(s_v.n)
        V_W2CAST = s_v.n

        @block.sync
        def _(sync):
            for t in range(32):
                if t < 2:
                    sync.wait_ge(s_s.h, S_W1CAST)
                else:
                    sync.wait_ge(s_v.h, w2cast[t - 2])
                s_w2.dma(sync.dma_start(
                    out=wld[:, (t % 2) * 1024:(t % 2 + 1) * 1024],
                    in_=w2e[t * 128:(t + 1) * 128, :]))

        gre_ld = []
        tr_done = []
        mm1_last = [None]
        mm2_last = [None]
        ys_ev = []
        gc_ev = []
        yst_st = []
        tr_prev2 = None
        u_ctr = [0]
        for grp in range(NGRP):
            for tq in range(GSZ):
                t = grp * GSZ + tq
                tp = t % 2

                @block.sync
                def _(sync, t=t, tp=tp):
                    if t == 0:
                        sync.wait_ge(s_cc.h, CC_A2A)
                    if t >= 2:
                        sync.wait_ge(s_s.h, tr_done[t - 2])
                        sync.wait_ge(s_v.h, gc_ev[t - 2])
                    s_ld.dma(sync.dma_start(
                        out=gt[:, tp * ROWW:(tp + 1) * ROWW],
                        in_=gre[t * 128:(t + 1) * 128, :]))
                gre_ld.append(s_ld.n)

                trv2 = []
                for d in range(8):
                    dp = d % 2

                    @block.tensor
                    def _(tensor, t=t, d=d, dp=dp, tp=tp, tq2=tr_prev2):
                        tensor.wait_ge(s_ld.h, gre_ld[t])
                        if d >= 2:
                            tensor.wait_ge(s_s.h, trv2[d - 2])
                        elif t >= 1:
                            tensor.wait_ge(s_s.h, tq2[6 + d])
                        s_p.inc(tensor.transpose(
                            p_tr[:, dp * 128:(dp + 1) * 128],
                            gt[:, tp * ROWW + d * 128:tp * ROWW + (d + 1) * 128],
                            identb[:, :]))
                    TR_U = s_p.n

                    @block.scalar
                    def _(scalar, TR_U=TR_U, d=d, dp=dp, tq=tq, grp=grp):
                        scalar.wait_ge(s_p.h, TR_U)
                        if grp >= 1 and tq == 0 and d == 0:
                            scalar.wait_ge(s_p.h, mm1_last[0])
                        s_s.inc(scalar.activation(
                            out=xfm[:, d * 256 + tq * 128:
                                    d * 256 + (tq + 1) * 128],
                            in_=p_tr[:, dp * 128:(dp + 1) * 128],
                            func=AF.Copy))
                    trv2.append(s_s.n)
                tr_prev2 = trv2
                tr_done.append(trv2[-1])

                @block.vector
                def _(vector, t=t, tp=tp, tq=tq):
                    vector.wait_ge(s_ld.h, gre_ld[t])
                    vector.tensor_tensor(
                        out=tmp8[:, :],
                        in0=gt[:, tp * ROWW + 1024:tp * ROWW + 1032],
                        in1=oneb[:, :], op=ALU.mult)
                    vector.drain()
                    s_v.inc(vector.tensor_reduce(out=gcolg[:, tq:tq + 1],
                                                 in_=tmp8[:, :], axis=AX.X,
                                                 op=ALU.add))
                gc_ev.append(s_v.n)

            gel_ev = []
            for m in range(32):
                mp = m % 2

                @block.tensor
                def _(tensor, m=m, mp=mp, grp=grp, trl=tr_done[-1]):
                    if m == 0:
                        tensor.wait_ge(s_s.h, trl)
                        tensor.wait_ge(s_s.h, S_W1CAST)
                    if m >= 2:
                        tensor.wait_ge(s_s.h, gel_ev[m - 2])
                    for d in range(8):
                        mm = tensor.matmul(
                            p_mm1[:, mp * 512:mp * 512 + 256],
                            w1sb[:, d * 4096 + m * 128:d * 4096 + (m + 1) * 128],
                            xfm[:, d * 256:(d + 1) * 256],
                            start=(d == 0), stop=(d == 7))
                        if d == 7:
                            s_p.inc(mm)
                MM1_U = s_p.n

                @block.scalar
                def _(scalar, MM1_U=MM1_U, m=m, mp=mp, grp=grp):
                    scalar.wait_ge(s_p.h, MM1_U)
                    if grp >= 1 and m == 0:
                        scalar.wait_ge(s_p.h, mm2_last[0])
                    s_s.inc(scalar.activation(
                        out=hsb[:, m * 256:(m + 1) * 256],
                        in_=p_mm1[:, mp * 512:mp * 512 + 256],
                        func=AF.Gelu, bias=b1sb[:, m:m + 1], scale=1.0))
                gel_ev.append(s_s.n)
            mm1_last[0] = s_p.n

            for tq in range(GSZ):
                t = grp * GSZ + tq
                tp = t % 2
                for dh in range(2):
                    u = u_ctr[0]
                    pu = u % 2

                    @block.tensor
                    def _(tensor, tq=tq, dh=dh, u=u, pu=pu, grp=grp,
                          gl=gel_ev[-1]):
                        if tq == 0 and dh == 0:
                            tensor.wait_ge(s_s.h, gl)
                            if grp == 0:
                                tensor.wait_ge(s_v.h, V_W2CAST)
                                tensor.wait_ge(s_v.h, V_PCAST)
                        if u >= 2:
                            tensor.wait_ge(s_v.h, ys_ev[u - 2])
                        tensor.matmul(p_mm2[:, pu * 512:(pu + 1) * 512],
                                      onesbc[0:1, :],
                                      b2b[0:1, dh * 512:(dh + 1) * 512],
                                      start=True, stop=False)
                        for m in range(32):
                            mm = tensor.matmul(
                                p_mm2[:, pu * 512:(pu + 1) * 512],
                                hsb[:, m * 256 + tq * 128:
                                    m * 256 + (tq + 1) * 128],
                                w2sb[:, m * 1024 + dh * 512:
                                     m * 1024 + (dh + 1) * 512],
                                start=False, stop=(m == 31))
                            if m == 31:
                                s_p.inc(mm)
                    MM2_U = s_p.n

                    @block.vector
                    def _(vector, MM2_U=MM2_U, tq=tq, dh=dh, tp=tp, pu=pu,
                          gc=gc_ev[grp * GSZ + tq], t=t):
                        vector.wait_ge(s_p.h, MM2_U)
                        vector.wait_ge(s_v.h, gc)
                        if t >= 2 and dh == 0:
                            vector.wait_ge(s_ys.h, 16 * (t - 1))
                        s_v.inc(vector.tensor_scalar(
                            out=ysb[:, tp * 1024 + dh * 512:
                                    tp * 1024 + (dh + 1) * 512],
                            in0=p_mm2[:, pu * 512:(pu + 1) * 512],
                            scalar1=gcolg[:, tq:tq + 1], scalar2=None,
                            op0=ALU.mult))
                    ys_ev.append(s_v.n)
                    u_ctr[0] += 1
                mm2_last[0] = s_p.n

                @block.sync
                def _(sync, t=t, tp=tp, ys=ys_ev[-1]):
                    sync.wait_ge(s_v.h, ys)
                    s_ys.dma(sync.dma_start(
                        out=yst[t * 128:(t + 1) * 128, :],
                        in_=ysb[:, tp * 1024:(tp + 1) * 1024]))
                yst_st.append(s_ys.n)
        YST_ALL = s_ys.n

        @block.gpsimd
        def _(g):
            g.wait_ge(s_ys.h, YST_ALL)
            from concourse.aot_env import allow_ici_all_to_all
            with allow_ici_all_to_all():
                s_cc.inc(g.collective_compute(
                    "AllToAll", ALU.bypass,
                    replica_groups=[list(range(CORES))],
                    ins=[yst.ap().opt()], outs=[yret.ap().opt()]))
        CC_A2B = s_cc.n

        MM2_FIN = mm2_last[0]
        yr_ld = []
        @block.sync
        def _(sync):
            sync.wait_ge(s_cc.h, CC_A2B)
            sync.wait_ge(s_p.h, MM2_FIN)
            for k in range(RTIL):
                s_ld.dma(sync.dma_start(
                    out=yrall[:, k * 1024:(k + 1) * 1024],
                    in_=yret[k * 128:(k + 1) * 128, :]))
                yr_ld.append(s_ld.n)

        @block.gpsimd
        def _(g):
            g.wait_ge(s_v.h, V_OFFC)
            s_io.inc(g.iota(slotc[:, 0:5], pattern=[[128, 5]], base=0,
                            channel_multiplier=1,
                            allow_small_or_imprecise_dtypes=True))
        IO_E = s_io.n

        oht_ev = []
        orw_ev = []
        res_st = []
        for tc in range(NTIL):
            par = tc % 2

            @block.sync
            def _(sync, tc=tc, par=par):
                if tc >= 2:
                    sync.wait_ge(s_v.h, res_st[tc - 2])
                s_ld.dma(sync.dma_start(
                    out=wld[:, 2048 + par * 1024:2048 + (par + 1) * 1024],
                    in_=xin[tc * 128:(tc + 1) * 128, :]))
            LD_RX = s_ld.n

            for ee in range(8):
                u = tc * 8 + ee
                up = u % 2
                po = up * 256

                @block.tensor
                def _(tensor, tc=tc, ee=ee, u=u, po=po):
                    if u == 0:
                        tensor.wait_ge(s_v.h, V_OFFC)
                    if u >= 2:
                        tensor.wait_ge(s_v.h, orw_ev[u - 2])
                    s_p.inc(tensor.transpose(
                        p_mix[0:1, po:po + 128],
                        selsc[:, tc * 8 + ee:tc * 8 + ee + 1],
                        identf[:, :]))
                TRO_U = s_p.n

                @block.vector
                def _(vector, TRO_U=TRO_U, up=up, po=po):
                    vector.wait_ge(s_p.h, TRO_U)
                    s_v.inc(vector.tensor_copy(
                        offrow[0:1, up * 128:(up + 1) * 128],
                        p_mix[0:1, po:po + 128]))
                ORW_U = s_v.n
                orw_ev.append(ORW_U)

                @block.tensor
                def _(tensor, ORW_U=ORW_U, up=up, po=po):
                    tensor.wait_ge(s_v.h, ORW_U)
                    s_p.inc(tensor.matmul(
                        p_mix[:, po + 128:po + 256],
                        ones512[0:1, 0:128],
                        offrow[0:1, up * 128:(up + 1) * 128],
                        start=True, stop=True))
                BCO_U = s_p.n

                @block.vector
                def _(vector, BCO_U=BCO_U, up=up, po=po, u=u):
                    vector.wait_ge(s_p.h, BCO_U)
                    if u == 0:
                        vector.wait_ge(s_io.h, IO_E)
                    if u >= 2:
                        vector.wait_ge(s_p.h, oht_ev[u - 2])
                    for sc in range(5):
                        if sc == 4:
                            vector.drain()
                        mmv = vector.tensor_scalar(
                            out=ohtb[:, up * 640 + sc * 128:
                                     up * 640 + (sc + 1) * 128],
                            in0=p_mix[:, po + 128:po + 256],
                            scalar1=slotc[:, sc:sc + 1], scalar2=None,
                            op0=ALU.is_equal)
                    s_v.inc(mmv)
                OHB_U = s_v.n

                @block.tensor
                def _(tensor, tc=tc, ee=ee, u=u, up=up, OHB_U=OHB_U):
                    tensor.wait_ge(s_v.h, OHB_U)
                    if tc == 0:
                        tensor.wait_ge(s_ld.h, yr_ld[ee * 5 + 4])
                    mm = None
                    for sc in range(5):
                        first = (ee == 0 and sc == 0)
                        last = (ee == 7 and sc == 4)
                        for dh in range(2):
                            mm = tensor.matmul(
                                p_mm1[:, dh * 512:(dh + 1) * 512],
                                ohtb[:, up * 640 + sc * 128:
                                     up * 640 + (sc + 1) * 128],
                                yrall[:, (ee * 5 + sc) * 1024 + dh * 512:
                                      (ee * 5 + sc) * 1024 + (dh + 1) * 512],
                                start=first, stop=last)
                    s_p.inc(mm)
                oht_ev.append(s_p.n)
            ACC_TC = s_p.n

            @block.vector
            def _(vector, ACC_TC=ACC_TC, par=par, LD_RX=LD_RX, tc=tc):
                vector.wait_ge(s_p.h, ACC_TC)
                vector.wait_ge(s_ld.h, LD_RX)
                s_v.inc(vector.tensor_tensor(
                    out=wld[:, 2048 + par * 1024:2048 + (par + 1) * 1024],
                    in0=wld[:, 2048 + par * 1024:2048 + (par + 1) * 1024],
                    in1=p_mm1[:, 0:1024], op=ALU.add))
            res_st.append(s_v.n)
            RES_I = s_v.n

            @block.sync
            def _(sync, RES_I=RES_I, par=par, tc=tc):
                sync.wait_ge(s_v.h, RES_I)
                s_st.dma(sync.dma_start(
                    out=xout[tc * 128:(tc + 1) * 128, :],
                    in_=wld[:, 2048 + par * 1024:2048 + (par + 1) * 1024]))
        ST_ALL = s_st.n

        @block.gpsimd
        def _(g):
            g.wait_ge(s_st.h, ST_ALL)

    return nc


def stage_l0_ffn_inputs(x1_full, inp):
    router = np.asarray(inp["L0_ffn_router_w"], np.float32)
    w1 = np.asarray(inp["L0_ffn_w1"], np.float32)
    b1 = np.asarray(inp["L0_ffn_b1"], np.float32)
    w2 = np.asarray(inp["L0_ffn_w2"], np.float32)
    b2 = np.asarray(inp["L0_ffn_b2"], np.float32)
    x = np.asarray(x1_full, np.float32).reshape(B * T, C)
    rwT = np.ascontiguousarray(router.T)
    maps = []
    for c in range(CORES):
        rows = owner_order_index(c)
        oh = np.zeros((128, 8), np.float32)
        oh[:, c] = 1.0
        maps.append(dict(
            xin=np.ascontiguousarray(x[rows]), rwt=rwT,
            w1e=np.ascontiguousarray(w1[c]),
            w2e=np.ascontiguousarray(w2[c]),
            b1e=np.ascontiguousarray(b1[c].reshape(32, 128).T),
            b2e=b2[c][None, :], onehot=oh))
    return maps


def unshard_rows(shards):
    out = np.zeros((B * T, C), dtype=np.float32)
    for c in range(CORES):
        out[owner_order_index(c)] = np.asarray(shards[c])
    return out.reshape(B, T, C)


def kernel(**inputs):
    x = np.asarray(inputs["L0_attn_x"], dtype=np.float32)

    ncA = build_l0_attn()
    resA = run_bass_kernel_spmd(ncA, stage_l0_attn_inputs(x, inputs),
                                core_ids=list(range(CORES)))
    x1 = unshard_l0_out([r["xout"] for r in resA.results])

    ncB = build_l0_ffn()
    resB = run_bass_kernel_spmd(ncB, stage_l0_ffn_inputs(x1, inputs),
                                core_ids=list(range(CORES)))
    x2 = unshard_rows([r["xout"] for r in resB.results])

    ncC = build_l1_attn()
    resC = run_bass_kernel_spmd(ncC, stage_l1_attn_inputs(shard_tokens(x2),
                                                          inputs),
                                core_ids=list(range(CORES)))
    x3 = unshard_tokens([r["xout"] for r in resC.results])

    ncD = build_l1_ffn()
    resD = run_bass_kernel_spmd(ncD, stage_l1_ffn_inputs(shard_tokens(x3),
                                                         inputs),
                                core_ids=list(range(CORES)))
    return unshard_tokens([r["xout"] for r in resD.results])
